# version 02
# baseline (speedup 1.0000x reference)
import numpy as np
import ml_dtypes
import concourse.bass as bass
import concourse.mybir as mybir
from concourse.bass_utils import run_bass_kernel_spmd

F32 = mybir.dt.float32
BF16 = mybir.dt.bfloat16
I32 = mybir.dt.int32
AF = mybir.ActivationFunctionType
ALU = mybir.AluOpType
AX = mybir.AxisListType

P = 128
T = 2048
D = 1024
NH = 16
SEG = 512
NTC = 16
NKC = 8
NE = 8
HF = 2048
EPS = 1e-5
MAGIC = 0x5F3759DF
ENGINES = ("sync", "gpsimd", "tensor", "vector", "scalar")


class KB:
    def __init__(self, nc):
        self.nc = nc
        self.q = {e: [] for e in ENGINES}
        self.cnt = {}
        self.sems = {}

    def new_sem(self, stack, name):
        self.sems[name] = stack.enter_context(self.nc.semaphore(name))
        self.cnt[name] = 0

    def op(self, eng, fn, waits=(), inc=None, amt=1):
        ws = tuple((s, v) for w in waits if w is not None
                   for (s, v) in [w] if v > 0)

        def emit(engine, ws=ws, fn=fn, inc=inc, amt=amt):
            for s, v in ws:
                engine.wait_ge(self.sems[s], v)
            inst = fn(engine)
            if inc is not None:
                inst.then_inc(self.sems[inc], amt)
        self.q[eng].append(emit)
        if inc is not None:
            self.cnt[inc] += amt
            return (inc, self.cnt[inc])
        return None

    def wait(self, eng, waits):
        ws = tuple((s, v) for w in waits if w is not None
                   for (s, v) in [w] if v > 0)
        if ws:
            def emit(engine, ws=ws):
                for s, v in ws:
                    engine.wait_ge(self.sems[s], v)
            self.q[eng].append(emit)

    def emit_all(self, block):
        for name in ENGINES:
            def body(engine, name=name):
                for emit in self.q[name]:
                    emit(engine)
            getattr(block, name)(body)


def build_nc(dbg=False, ne_dbg=NE):
    nc = bass.Bass("TRN2", target_bir_lowering=False)
    x_h = nc.dram_tensor("x", [T, D], F32, kind="ExternalInput").ap()
    wqkv_h = nc.dram_tensor("wqkv", [2, 3 * D, D], F32, kind="ExternalInput").ap()
    bqkv_h = nc.dram_tensor("bqkv", [2, 3 * D], F32, kind="ExternalInput").ap()
    wout_h = nc.dram_tensor("wout", [2, D, D], F32, kind="ExternalInput").ap()
    bout_h = nc.dram_tensor("bout", [2, D], F32, kind="ExternalInput").ap()
    gw_h = nc.dram_tensor("gate_w", [D, NE], F32, kind="ExternalInput").ap()
    w1_h = nc.dram_tensor("w1", [NE, D, HF], F32, kind="ExternalInput").ap()
    b1_h = nc.dram_tensor("b1", [NE, HF], F32, kind="ExternalInput").ap()
    l1g_h = nc.dram_tensor("ln1g", [NE, HF], F32, kind="ExternalInput").ap()
    l1b_h = nc.dram_tensor("ln1b", [NE, HF], F32, kind="ExternalInput").ap()
    w2_h = nc.dram_tensor("w2", [NE, HF, D], F32, kind="ExternalInput").ap()
    b2_h = nc.dram_tensor("b2", [NE, D], F32, kind="ExternalInput").ap()
    lng_h = nc.dram_tensor("lng", [1, D], F32, kind="ExternalInput").ap()
    lnb_h = nc.dram_tensor("lnb", [1, D], F32, kind="ExternalInput").ap()
    id_h = nc.dram_tensor("ident", [P, P], F32, kind="ExternalInput").ap()
    out_h = nc.dram_tensor("out", [T, D], F32, kind="ExternalOutput").ap()
    x2o_h = nc.dram_tensor("x2dbg", [T, D], F32, kind="ExternalOutput").ap()
    dbg_h = {}
    if dbg == 2:
        for nm, shape, dt in [("d_call", [P, P], F32), ("d_ln2", [D, T], BF16),
                              ("d_mu2", [P, NTC], F32), ("d_rs2", [P, NTC], F32),
                              ("d_yacc", [P, 8192], F32), ("d_lngr", [P, D], F32),
                              ("d_lnbr", [P, D], F32), ("d_b1r", [P, HF], F32),
                              ("d_gr", [P, HF], F32), ("d_brr", [P, HF], F32),
                              ("d_hsb0", [P, HF], F32), ("d_hbf0", [P, HF], BF16),
                              ("d_hfm0", [P, HF], BF16), ("d_w2e", [P, 16 * 1024], BF16),
                              ("d_w1e", [P, 16 * 1024], BF16)]:
            dbg_h[nm] = nc.dram_tensor(nm, shape, dt, kind="ExternalOutput").ap()
    if dbg == 1:
        for nm, shape, dt in [("d_lnar", [D, T], BF16), ("d_qf", [P, 4096], BF16),
                              ("d_kf", [P, 4096], BF16), ("d_vt", [P, 4096], BF16),
                              ("d_otm", [P, 4096], BF16), ("d_ofm0", [D, D], BF16),
                              ("d_ofm1", [D, D], BF16), ("d_zt", [P, 16], F32),
                              ("d_wo0", [D, D], BF16), ("d_wo1", [D, D], BF16)]:
            dbg_h[nm] = nc.dram_tensor(nm, shape, dt, kind="ExternalOutput").ap()
    wqkvb_h = nc.dram_tensor("wqkvb", [2, 3 * D, D], BF16, kind="Internal").ap()
    woutb_h = nc.dram_tensor("woutb", [2, D, D], BF16, kind="Internal").ap()
    x2_h = nc.dram_tensor("x2scr", [T, D], F32, kind="Internal").ap()

    xp_h = x_h.rearrange("(c j par) d -> c par j d", j=64, par=2)
    x2p_h = x2_h.rearrange("(c j par) d -> c par j d", j=64, par=2)

    from contextlib import ExitStack
    with ExitStack() as stack:
        def sbt(name, shape, dt):
            return stack.enter_context(nc.sbuf_tensor(name, shape, dt))

        def pstt(name, shape, dt):
            return stack.enter_context(nc.psum_tensor(name, shape, dt))

        idb = sbt("idb", [P, P], BF16)
        idf = sbt("idf", [P, P], F32)
        ones_r = sbt("ones_r", [1, P], F32)
        onec = sbt("onec", [P, 1], F32)
        bstg = sbt("bstg", [24, P], F32)
        bq = sbt("bq", [P, 48], F32)
        c_all = sbt("c_all", [P, P], F32)
        mu2 = sbt("mu2", [P, NTC], F32)
        rs2 = sbt("rs2", [P, NTC], F32)
        gwt = sbt("gwt", [P, 64], F32)
        csr = sbt("csr", [P, NE], F32)
        cs_row = sbt("cs_row", [1, NE], F32)
        U8 = sbt("U8", [8, 2048], F32)
        sm_a = sbt("sm_a", [P, 16], F32)
        sm_b = sbt("sm_b", [P, 16], F32)
        sm_c = sbt("sm_c", [P, 16], F32)
        sm_i = sbt("sm_i", [P, 16], I32)
        bn_raw = sbt("bn_raw", [P, 24], F32)
        bn_ag = sbt("bn_ag", [P, 2], F32)
        g8_a = sbt("g8_a", [P, NE], F32)
        g8_b = sbt("g8_b", [P, NE], F32)
        g8_d = sbt("g8_d", [P, NE], F32)
        zt = sbt("zt", [P, 16], F32)
        lntm = sbt("lntm", [P, D], BF16)
        lnar = [sbt(f"lnar{k}", [P, T], BF16) for k in range(NKC)]
        arw = [sbt(f"arw{k}", [P, 4096], BF16) for k in range(NKC)]
        arx = [sbt(f"arx{k}", [P, 4096], BF16) for k in range(12)]

        wqkT = [arw[k][:, 0:3072] for k in range(NKC)]
        woT = [[arw[k][:, 3072:4096] for k in range(NKC)],
               [arx[10 + j // 4][:, (j % 4) * 1024:(j % 4 + 1) * 1024] for j in range(NKC)]]
        ofm = [[arx[0 + j // 4][:, (j % 4) * 1024:(j % 4 + 1) * 1024] for j in range(NKC)],
               [arx[2 + j // 4][:, (j % 4) * 1024:(j % 4 + 1) * 1024] for j in range(NKC)]]
        qf = arx[4]
        kf = arx[5]
        vt = arx[6]
        otm = arx[7]
        et = [arx[8][:, 0:2048], arx[8][:, 2048:4096]]
        bvr = arx[9][:, 0:2048].bitcast(F32)
        bo3 = bvr
        xst = arx[9][:, 2048:4096].bitcast(F32)
        wtmp = arx[7][:, 0:2048].bitcast(F32)
        yacc = [arx[j // 2][:, (j % 2) * 2048:(j % 2 + 1) * 2048].bitcast(F32) for j in range(8)]
        w1e = [arw[k][:, 0:2048] for k in range(NKC)]
        w2e = [arw[j // 2][:, 2048 + (j % 2) * 1024:2048 + (j % 2 + 1) * 1024] for j in range(16)]
        b1r = arx[4][:, :].bitcast(F32)
        gr_ = arx[5][:, :].bitcast(F32)
        brr = arx[6][:, :].bitcast(F32)
        x2t = arx[7][:, 0:2048].bitcast(F32)
        hsb = [arx[8][:, :].bitcast(F32), arx[9][:, :].bitcast(F32)]
        hbf = [arx[11][:, 0:2048], arx[11][:, 2048:4096]]
        hfm = [arx[10][:, 0:2048], arx[10][:, 2048:4096]]
        rowb = U8[0:1, 0:2048]
        cta = U8[:, 0:1024]
        b2t = U8[0:8, 1024:2048]
        lngr = arx[10][:, 0:2048].bitcast(F32)
        lnbr = arx[10][:, 2048:4096].bitcast(F32)
        outsc = [arx[11][:, 0:2048].bitcast(F32), arx[11][:, 2048:4096].bitcast(F32)]

        PA = pstt("PA", [P, 1024], F32)
        PB = pstt("PB", [P, 1024], F32)
        PC = pstt("PC", [P, 1024], F32)
        PD = pstt("PD", [P, 512], F32)
        PT = pstt("PT", [P, 1024], BF16)

        kb = KB(nc)
        for s in ("dX", "dW", "dW1", "dW2", "dL", "dR", "dT", "T", "V", "A"):
            kb.new_sem(stack, s)
        block = stack.enter_context(nc.Block())
        O = kb.op

        def DX(fn, waits=()):
            return O("sync", fn, waits, inc="dX", amt=16)

        def DW(fn, waits=()):
            return O("gpsimd", fn, waits, inc="dW", amt=16)

        def DL(fn, waits=()):
            return O("sync", fn, waits, inc="dL", amt=16)

        last_dr = [None]

        def DR(fn, waits=()):
            tk = O("sync", fn, list(waits) + [last_dr[0]], inc="dR", amt=16)
            last_dr[0] = tk
            return tk

        def DT(fn, waits=()):
            return O("sync", fn, waits, inc="dT", amt=16)

        pd_free = [None]

        def newton_rsqrt(var_ap, out_ap, n, wait=()):
            vpe = sm_a[:, 0:n]
            t2 = sm_b[:, 0:n]
            ii = sm_i[:, 0:n]
            tk = O("vector", lambda e: e.tensor_scalar(vpe, var_ap, EPS, None, op0=ALU.add), wait, inc="V")
            tk = O("vector", lambda e: e.tensor_scalar(ii, vpe.bitcast(I32), 1, None, op0=ALU.arith_shift_right), [tk], inc="V")
            tk = O("vector", lambda e: e.tensor_scalar(ii, ii, -1.0, float(MAGIC), op0=ALU.mult, op1=ALU.add), [tk], inc="V")
            tk = O("vector", lambda e: e.tensor_copy(out_ap, ii.bitcast(F32)), [tk], inc="V")
            for _ in range(2):
                tk = O("vector", lambda e: e.tensor_tensor(t2, out_ap, out_ap, op=ALU.mult), [tk], inc="V")
                tk = O("vector", lambda e: e.tensor_tensor(t2, t2, vpe, op=ALU.mult), [tk], inc="V")
                tk = O("vector", lambda e: e.tensor_scalar(t2, t2, -0.5, 1.5, op0=ALU.mult, op1=ALU.add), [tk], inc="V")
                tk = O("vector", lambda e: e.tensor_tensor(out_ap, out_ap, t2, op=ALU.mult), [tk], inc="V")
            return tk

        def ln_stats(src_ap, width, mu_ap, rstd_ap, wait=()):
            ng = width // 512
            tk = None
            for i in range(ng):
                tk = O("vector", lambda e, i=i: e.bn_stats(
                    bn_raw[:, 6 * i:6 * i + 6], src_ap[:, 512 * i:512 * (i + 1)]), wait, inc="V")
            tk = O("vector", lambda e, ng=ng: e.bn_aggr(bn_ag[:, :], bn_raw[:, 0:6 * ng]), [tk], inc="V")
            tk = O("vector", lambda e: e.tensor_copy(mu_ap, bn_ag[:, 0:1]), [tk], inc="V")
            return newton_rsqrt(bn_ag[:, 1:2], rstd_ap, 1, wait=[tk])

        def repl_rows(dst_f32, row_ap, width, wait=()):
            tk_out = None
            for j in range(0, width, 512):
                w = min(512, width - j)
                tm = O("tensor", lambda e, j=j, w=w: e.matmul(
                    PD[:, 0:w], ones_r[:, :], row_ap[:, j:j + w], start=True, stop=True),
                    list(wait) + [pd_free[0]], inc="T")
                tk_out = O("vector", lambda e, j=j, w=w: e.tensor_copy(
                    dst_f32[:, j:j + w], PD[:, 0:w]), [tm], inc="V")
                pd_free[0] = tk_out
            return tk_out

        t_ones = O("vector", lambda e: e.memset(ones_r[:, :], 1.0), inc="V")
        t_onec = O("vector", lambda e: e.memset(onec[:, :], 1.0), inc="V")
        t_id = DR(lambda e: e.dma_start(out=idf[:, :], in_=id_h[:, :]))
        t_idb = DW(lambda e: e.dma_start(out=idb[:, :], in_=id_h[:, :]))
        t_wqc = [DW(lambda e, b=b: e.dma_start(out=wqkvb_h[b], in_=wqkv_h[b])) for b in range(2)]
        t_woc = [DW(lambda e, b=b: e.dma_start(out=woutb_h[b], in_=wout_h[b])) for b in range(2)]
        t_bq = []
        tprev = None
        for br in range(2):
            tl = DR(lambda e, br=br: e.dma_start(
                out=bstg[:, :], in_=bqkv_h[br].rearrange("(c p) -> c p", p=128)),
                [tprev])
            tt_ = O("tensor", lambda e: e.transpose(PD[:, 0:24], bstg[:, :], idf[0:24, 0:24]),
                    [tl, t_id, pd_free[0]], inc="T")
            tk = O("vector", lambda e, br=br: e.tensor_copy(bq[:, 24 * br:24 * br + 24], PD[:, 0:24]),
                   [tt_], inc="V")
            pd_free[0] = tk
            tprev = tk
            t_bq.append(tk)
        t_gw = DR(lambda e: e.dma_start(
            out=gwt[:, :].rearrange("p (c e) -> p c e", e=NE),
            in_=gw_h.rearrange("(c p) e -> p c e", p=128)))
        tm = None
        for kc in range(NKC):
            tm = O("tensor", lambda e, kc=kc: e.matmul(
                PD[0:1, 0:NE], onec[:, :], gwt[:, 8 * kc:8 * kc + 8],
                start=(kc == 0), stop=(kc == NKC - 1)),
                [t_gw, t_onec, pd_free[0]], inc="T" if kc == NKC - 1 else None)
        t_csrow = O("vector", lambda e: e.tensor_copy(cs_row[:, :], PD[0:1, 0:NE]), [tm], inc="V")
        pd_free[0] = t_csrow
        tm = O("tensor", lambda e: e.matmul(PD[:, 0:NE], ones_r[:, :], cs_row[:, :],
                                            start=True, stop=True), [t_csrow, t_ones, pd_free[0]], inc="T")
        t_csr = O("vector", lambda e: e.tensor_copy(csr[:, :], PD[:, 0:NE]), [tm], inc="V")
        pd_free[0] = t_csr

        t_lnar = None
        t_arw_free = None
        t_ofm_free = None
        t_xst_free = None
        t_wtmp_free = None
        t_x2_done = None
        pa_free = [None, None]
        pb_free = [None, None]
        pc_free = [None, None]
        pt_free = [None]
        t_et_free = [None, None]
        t_otr_done = None
        t_wq_dma = [None] * NKC
        t_bvr = None
        t_woT_ready = None
        t_wo_free = [None, None]

        for span in range(2):
            base = span * 1024
            for i in range(8):
                tc = span * 8 + i
                tl = DL(lambda e, tc=tc: e.dma_start(
                    out=xst[:, :], in_=x_h[128 * tc:128 * (tc + 1), :]), [t_xst_free])
                ts_ = ln_stats(xst, D, sm_c[:, 0:1], sm_c[:, 1:2], wait=[tl])
                ta = O("vector", lambda e: e.tensor_scalar(
                    lntm[:, :], xst[:, :], sm_c[:, 0:1], sm_c[:, 1:2],
                    op0=ALU.subtract, op1=ALU.mult), [ts_], inc="V")
                t_xst_free = ta
                tmm = None
                for kc in range(NKC):
                    tmm = O("tensor", lambda e, kc=kc: e.transpose(
                        PT[:, 128 * kc:128 * (kc + 1)], lntm[:, 128 * kc:128 * (kc + 1)], idb[:, :]),
                        [ta, t_idb, pt_free[0]], inc="T" if kc == NKC - 1 else None)
                tev = None
                for kc in range(NKC):
                    tev = O("vector", lambda e, kc=kc, tc=tc: e.tensor_copy(
                        lnar[kc][:, 128 * tc:128 * (tc + 1)], PT[:, 128 * kc:128 * (kc + 1)]),
                        [tmm], inc="V")
                pt_free[0] = tev
                t_lnar = tev

            for br in range(2):
                for kc in range(NKC):
                    t_wq_dma[kc] = DT(lambda e, kc=kc, br=br: e.dma_start(
                        out=wqkT[kc][:, :], in_=wqkvb_h[br][:, 128 * kc:128 * (kc + 1)],
                        transpose=True), [t_wqc[br], t_arw_free])
                if span == 0:
                    for kc in range(NKC):
                        t_woT_ready = DT(lambda e, kc=kc, br=br: e.dma_start(
                            out=woT[br][kc][:, :], in_=woutb_h[br][:, 128 * kc:128 * (kc + 1)],
                            transpose=True), [t_woc[br]])
                tl = DR(lambda e, br=br: e.dma_start(
                    out=rowb[:, 0:1024], in_=bqkv_h[br:br + 1, 2048:3072]), [t_bvr])
                t_bvr = repl_rows(bvr, rowb[:, 0:1024], 1024, wait=[tl])

                for seg in range(2):
                    if br == 0:
                        co, st_ = base + SEG * seg, 1
                    else:
                        co, st_ = base + seg, 2

                    t_qk_ready = None
                    for fc in range(16):
                        half = fc % 2
                        pa = PA[:, 512 * half:512 * (half + 1)]
                        tmm = None
                        for kc in range(NKC):
                            rhs_ap = lnar[kc][:, co:co + st_ * (SEG - 1) + 1:st_]
                            tmm = O("tensor", lambda e, kc=kc, fc=fc, pa=pa, rhs_ap=rhs_ap: e.matmul(
                                pa, wqkT[kc][:, 128 * fc:128 * (fc + 1)], rhs_ap,
                                start=(kc == 0), stop=(kc == NKC - 1)),
                                [t_wq_dma[NKC - 1], t_lnar, pa_free[half]],
                                inc="T" if kc == NKC - 1 else None)
                        dst = qf if fc < 8 else kf
                        dco = 512 * (fc % 8)
                        t_qk_ready = O("vector", lambda e, pa=pa, dst=dst, dco=dco, fc=fc, br=br: e.tensor_scalar(
                            dst[:, dco:dco + 512], pa, bq[:, 24 * br + fc:24 * br + fc + 1], None,
                            op0=ALU.add), [tmm, t_bq[br]], inc="V")
                        pa_free[half] = t_qk_ready
                    t_v_ready = None
                    for j in range(4):
                        tmm = None
                        for kc in range(NKC):
                            lhs_ap = lnar[kc][:, co + st_ * 128 * j:co + st_ * (128 * (j + 1) - 1) + 1:st_]
                            for g in range(2):
                                tmm = O("tensor", lambda e, kc=kc, g=g, lhs_ap=lhs_ap: e.matmul(
                                    PB[:, 512 * g:512 * (g + 1)], lhs_ap,
                                    wqkT[kc][:, 2048 + 512 * g:2048 + 512 * (g + 1)],
                                    start=(kc == 0), stop=(kc == NKC - 1)),
                                    [t_wq_dma[NKC - 1], t_lnar, pb_free[g]],
                                    inc="T" if (kc == NKC - 1 and g == 1) else None)
                        t_v_ready = O("vector", lambda e, j=j: e.tensor_tensor(
                            vt[:, 1024 * j:1024 * (j + 1)], PB[:, :], bvr, op=ALU.add),
                            [tmm, t_bvr], inc="V")
                        pb_free[0] = pb_free[1] = t_v_ready
                    t_arw_free = tmm

                    def emit_scores(h, t_qk_ready=t_qk_ready):
                        pi = 8 * (h % 2)
                        hp = 64 * (h % 2)
                        qh = qf[hp:hp + 64, 512 * (h // 2):512 * (h // 2) + 512]
                        kh = kf[hp:hp + 64, 512 * (h // 2):512 * (h // 2) + 512]
                        tS = []
                        for qc in range(4):
                            half = qc % 2
                            pa = PA[:, 512 * half:512 * (half + 1)]
                            tm1 = O("tensor", lambda e, qh=qh, kh=kh, qc=qc, pa=pa: e.matmul(
                                pa, qh[:, 128 * qc:128 * (qc + 1)], kh, start=True, stop=True),
                                [t_qk_ready, pa_free[half]], inc="T")
                            te = O("scalar", lambda e, pa=pa, qc=qc, pi=pi: e.activation(
                                pa, pa, AF.Exp, bias=0.0, scale=0.125,
                                accum_out=zt[:, pi + qc:pi + qc + 1]), [tm1], inc="A")
                            pa_free[half] = te
                            tS.append(te)
                        tc_ = []
                        for qc in range(4):
                            t1 = O("vector", lambda e, qc=qc, pi=pi: e.tensor_scalar(
                                zt[:, pi + qc:pi + qc + 1], zt[:, pi + qc:pi + qc + 1], 3.0, None,
                                op0=ALU.mult), [tS[qc]], inc="V")
                            t2 = O("vector", lambda e, qc=qc, pi=pi: e.reciprocal(
                                zt[:, pi + 4 + qc:pi + 5 + qc], zt[:, pi + qc:pi + qc + 1]),
                                [t1], inc="V")
                            tc_.append(t2)
                        eb = et[h % 2]
                        tE = None
                        for kc in range(4):
                            half = kc % 2
                            pb = PB[:, 512 * half:512 * (half + 1)]
                            tm1 = O("tensor", lambda e, qh=qh, kh=kh, kc=kc, pb=pb: e.matmul(
                                pb, kh[:, 128 * kc:128 * (kc + 1)], qh, start=True, stop=True),
                                [t_qk_ready, pb_free[half], t_et_free[h % 2]], inc="T")
                            tE = O("scalar", lambda e, pb=pb, eb=eb, kc=kc: e.activation(
                                eb[:, 512 * kc:512 * (kc + 1)], pb, AF.Exp, bias=0.0, scale=0.125),
                                [tm1], inc="A")
                            pb_free[half] = tE
                        return (h, tc_, tE)

                    def emit_pv(state, t_v_ready=t_v_ready):
                        h, tc_, tE = state
                        pi = 8 * (h % 2)
                        eb = et[h % 2]
                        last_ev = None
                        tmm = None
                        for qc in range(4):
                            half = qc % 2
                            pcs = PC[:, 512 * half:512 * half + 64]
                            tmm = None
                            for kc in range(4):
                                tmm = O("tensor", lambda e, qc=qc, kc=kc, h=h, eb=eb, pcs=pcs: e.matmul(
                                    pcs, eb[:, 512 * kc + 128 * qc:512 * kc + 128 * (qc + 1)],
                                    vt[:, 1024 * kc + 64 * h:1024 * kc + 64 * (h + 1)],
                                    start=(kc == 0), stop=(kc == 3)),
                                    [tE, t_v_ready, pc_free[half]], inc="T" if kc == 3 else None)
                            last_ev = O("vector", lambda e, qc=qc, h=h, pcs=pcs, pi=pi: e.tensor_scalar(
                                otm[:, 1024 * qc + 64 * h:1024 * qc + 64 * (h + 1)], pcs,
                                zt[:, pi + 4 + qc:pi + 5 + qc], None, op0=ALU.mult),
                                [tmm, tc_[qc], t_otr_done], inc="V")
                            pc_free[half] = last_ev
                        t_et_free[h % 2] = tmm
                        return last_ev

                    prev = None
                    t_seg_oevac = None
                    for h in range(NH):
                        st8 = emit_scores(h)
                        if prev is not None:
                            t_seg_oevac = emit_pv(prev)
                        prev = st8
                    t_seg_oevac = emit_pv(prev)

                    for fc in range(NKC):
                        tmm = None
                        for qc in range(4):
                            tmm = O("tensor", lambda e, fc=fc, qc=qc: e.transpose(
                                PT[:, 128 * qc:128 * (qc + 1)],
                                otm[:, 1024 * qc + 128 * fc:1024 * qc + 128 * (fc + 1)], idb[:, :]),
                                [t_seg_oevac, pt_free[0]], inc="T" if qc == 3 else None)
                        if br == 0:
                            dst = ofm[0][fc][:, 512 * seg:512 * (seg + 1)].rearrange(
                                "p (b par j) -> p b j par", b=4, par=2, j=64)
                        else:
                            dst = ofm[1][fc].rearrange(
                                "p (jhi par jlo) -> p jhi par jlo", jhi=8, par=2, jlo=64)[:, :, seg, :]
                        tev = O("vector", lambda e, dst=dst: e.tensor_copy(
                            dst, PT[:, 0:512]), [tmm], inc="V")
                        pt_free[0] = tev
                    t_otr_done = tev

            tl0 = DR(lambda e: e.dma_start(out=rowb[:, 0:1024], in_=bout_h[0:1, :]), [t_bvr])
            tl1 = DR(lambda e: e.dma_start(out=rowb[:, 1024:2048], in_=bout_h[1:2, :]))
            tr0 = O("vector", lambda e: e.tensor_tensor(
                rowb[:, 0:1024], rowb[:, 0:1024], rowb[:, 1024:2048], op=ALU.add),
                [tl0, tl1], inc="V")
            tr0 = O("vector", lambda e: e.tensor_scalar(
                rowb[:, 0:1024], rowb[:, 0:1024], 1.0 / 3.0, None, op0=ALU.mult), [tr0], inc="V")
            t_bo3 = repl_rows(bo3, rowb[:, 0:1024], 1024, wait=[tr0])
            t_bvr = t_bo3

            tm0 = tm1 = None
            for i in range(8):
                tc = span * 8 + i
                tl = DL(lambda e, tc=tc: e.dma_start(
                    out=xst[:, :], in_=xp_h[tc]),
                    [t_xst_free])
                for kc in range(NKC):
                    l0 = ofm[0][kc][:, 128 * i:128 * (i + 1)]
                    for g in range(2):
                        tm0 = O("tensor", lambda e, kc=kc, g=g, l0=l0: e.matmul(
                            PA[:, 512 * g:512 * (g + 1)], l0, woT[0][kc][:, 512 * g:512 * (g + 1)],
                            start=(kc == 0), stop=(kc == NKC - 1)),
                            [t_otr_done, pa_free[g], t_woT_ready],
                            inc="T" if (kc == NKC - 1 and g == 1) else None)
                for kc in range(NKC):
                    l1 = ofm[1][kc][:, 128 * i:128 * (i + 1)]
                    for g in range(2):
                        tm1 = O("tensor", lambda e, kc=kc, g=g, l1=l1: e.matmul(
                            PB[:, 512 * g:512 * (g + 1)], l1, woT[1][kc][:, 512 * g:512 * (g + 1)],
                            start=(kc == 0), stop=(kc == NKC - 1)),
                            [t_otr_done, pb_free[g], t_woT_ready],
                            inc="T" if (kc == NKC - 1 and g == 1) else None)
                tv = O("vector", lambda e: e.tensor_tensor(wtmp[:, :], PA[:, :], bo3, op=ALU.add),
                       [tm0, t_bo3, t_wtmp_free], inc="V")
                pa_free[0] = pa_free[1] = tv
                tv = O("vector", lambda e: e.tensor_tensor(wtmp[:, :], wtmp[:, :], PB[:, :], op=ALU.add),
                       [tm1, tv], inc="V")
                pb_free[0] = pb_free[1] = tv
                tv = O("vector", lambda e: e.tensor_tensor(wtmp[:, :], wtmp[:, :], xst[:, :], op=ALU.add),
                       [tl, tv], inc="V")
                t_xst_free = tv
                tst = DX(lambda e, tc=tc: e.dma_start(
                    out=x2p_h[tc], in_=wtmp[:, :]), [tv])
                t_wtmp_free = tst
                t_x2_done = tst
            t_ofm_free = tm1
            t_wo_free = [tm0, tm1]

        if dbg == 1:
            for k in range(NKC):
                DX(lambda e, k=k: e.dma_start(out=dbg_h["d_lnar"][128 * k:128 * (k + 1), :],
                                              in_=lnar[k][:, :]), [t_x2_done])
            DX(lambda e: e.dma_start(out=dbg_h["d_qf"][:, :], in_=qf[:, :]), [t_x2_done])
            DX(lambda e: e.dma_start(out=dbg_h["d_kf"][:, :], in_=kf[:, :]), [t_x2_done])
            DX(lambda e: e.dma_start(out=dbg_h["d_vt"][:, :], in_=vt[:, :]), [t_x2_done])
            DX(lambda e: e.dma_start(out=dbg_h["d_otm"][:, :], in_=otm[:, :]), [t_x2_done])
            for k in range(NKC):
                DX(lambda e, k=k: e.dma_start(out=dbg_h["d_ofm0"][128 * k:128 * (k + 1), :],
                                              in_=ofm[0][k][:, :]), [t_x2_done])
                DX(lambda e, k=k: e.dma_start(out=dbg_h["d_ofm1"][128 * k:128 * (k + 1), :],
                                              in_=ofm[1][k][:, :]), [t_x2_done])
            DX(lambda e: e.dma_start(out=dbg_h["d_zt"][:, :], in_=zt[:, :]), [t_x2_done])
            for k in range(NKC):
                DX(lambda e, k=k: e.dma_start(out=dbg_h["d_wo0"][128 * k:128 * (k + 1), :],
                                              in_=woT[0][k][:, :]), [t_x2_done])
                DX(lambda e, k=k: e.dma_start(out=dbg_h["d_wo1"][128 * k:128 * (k + 1), :],
                                              in_=woT[1][k][:, :]), [t_x2_done])
            DX(lambda e: e.dma_start(out=x2o_h[:, :], in_=x2_h[:, :]), [t_x2_done])
            kb.wait("sync", [("dX", kb.cnt["dX"]), ("dL", kb.cnt["dL"]),
                             ("dR", kb.cnt["dR"]), ("dT", kb.cnt["dT"])])
            kb.wait("gpsimd", [("dW", kb.cnt["dW"])])
            kb.emit_all(block)
            return nc
        t_call = None
        t_lnar2 = None
        t_x2t_free = t_wtmp_free

        for tc in range(NTC):
            tl = DL(lambda e, tc=tc: e.dma_start(
                out=xst[:, :], in_=x2_h[128 * tc:128 * (tc + 1), :]),
                [t_xst_free, t_x2_done])
            ts_ = ln_stats(xst, D, mu2[:, tc:tc + 1], rs2[:, tc:tc + 1], wait=[tl])
            ta = O("vector", lambda e, tc=tc: e.tensor_scalar(
                lntm[:, :], xst[:, :], mu2[:, tc:tc + 1], rs2[:, tc:tc + 1],
                op0=ALU.subtract, op1=ALU.mult), [ts_], inc="V")
            tmm = None
            for kc in range(NKC):
                tmm = O("tensor", lambda e, kc=kc: e.transpose(
                    PT[:, 128 * kc:128 * (kc + 1)], lntm[:, 128 * kc:128 * (kc + 1)], idb[:, :]),
                    [ta, pt_free[0]], inc="T" if kc == NKC - 1 else None)
            tev = None
            for kc in range(NKC):
                tev = O("vector", lambda e, kc=kc, tc=tc: e.tensor_copy(
                    lnar[kc][:, 128 * tc:128 * (tc + 1)], PT[:, 128 * kc:128 * (kc + 1)]),
                    [tmm], inc="V")
            pt_free[0] = tev
            t_lnar2 = tev
            t_x2t_ready = None
            for g in range(2):
                tmm = None
                for kc in range(4 * g, 4 * g + 4):
                    tmm = O("tensor", lambda e, kc=kc, g=g: e.transpose(
                        PC[:, 512 * g + 128 * (kc % 4):512 * g + 128 * (kc % 4 + 1)],
                        xst[:, 128 * kc:128 * (kc + 1)], idf[:, :]),
                        [ts_, pc_free[g], t_x2t_free], inc="T" if kc % 4 == 3 else None)
                t_x2t_ready = O("vector", lambda e, g=g: e.tensor_copy(
                    x2t[:, 512 * g:512 * (g + 1)], PC[:, 512 * g:512 * (g + 1)]), [tmm], inc="V")
                pc_free[g] = t_x2t_ready
            t_xst_free = tmm
            tmm = None
            for kc in range(NKC):
                tmm = O("tensor", lambda e, kc=kc: e.matmul(
                    PD[:, 0:NE], x2t[:, 128 * kc:128 * (kc + 1)], gwt[:, 8 * kc:8 * kc + 8],
                    start=(kc == 0), stop=(kc == NKC - 1)),
                    [t_x2t_ready, t_gw, pd_free[0]], inc="T" if kc == NKC - 1 else None)
            t_x2t_free = tmm
            tk = O("vector", lambda e, tc=tc: e.tensor_scalar(
                g8_a[:, :], csr[:, :], mu2[:, tc:tc + 1], None, op0=ALU.mult),
                [t_csr, ts_], inc="V")
            tk = O("vector", lambda e: e.tensor_tensor(g8_b[:, :], PD[:, 0:NE], g8_a[:, :],
                                                       op=ALU.subtract), [tmm, tk], inc="V")
            pd_free[0] = tk
            tk = O("vector", lambda e, tc=tc: e.tensor_scalar(
                g8_b[:, :], g8_b[:, :], rs2[:, tc:tc + 1], None, op0=ALU.mult), [tk], inc="V")
            tk = O("vector", lambda e: e.reduce_max(sm_c[:, 0:1], g8_b[:, :], axis=AX.X), [tk], inc="V")
            tk = O("vector", lambda e: e.tensor_scalar(g8_a[:, :], g8_b[:, :], sm_c[:, 0:1], None,
                                                       op0=ALU.is_equal), [tk], inc="V")
            tk = O("vector", lambda e: e.scalar_tensor_tensor(
                g8_d[:, :], g8_a[:, :], -1e30, g8_b[:, :], op0=ALU.mult, op1=ALU.add), [tk], inc="V")
            tk = O("vector", lambda e: e.reduce_max(sm_c[:, 1:2], g8_d[:, :], axis=AX.X), [tk], inc="V")
            tk = O("vector", lambda e: e.tensor_tensor(sm_c[:, 2:3], sm_c[:, 0:1], sm_c[:, 1:2],
                                                       op=ALU.subtract), [tk], inc="V")
            tk = O("scalar", lambda e: e.activation(sm_c[:, 3:4], sm_c[:, 2:3], AF.Exp,
                                                    bias=0.0, scale=-1.0), [tk], inc="A")
            tk = O("vector", lambda e: e.tensor_scalar(sm_c[:, 3:4], sm_c[:, 3:4], 1.0, None,
                                                       op0=ALU.add), [tk], inc="V")
            tk = O("vector", lambda e: e.reciprocal(sm_c[:, 4:5], sm_c[:, 3:4]), [tk], inc="V")
            tk = O("vector", lambda e: e.tensor_scalar(sm_c[:, 5:6], sm_c[:, 4:5], -1.0, 1.0,
                                                       op0=ALU.mult, op1=ALU.add), [tk], inc="V")
            tk = O("vector", lambda e: e.tensor_scalar(g8_d[:, :], g8_b[:, :], sm_c[:, 1:2], None,
                                                       op0=ALU.is_equal), [tk], inc="V")
            tk = O("vector", lambda e: e.tensor_scalar(g8_a[:, :], g8_a[:, :], sm_c[:, 4:5], None,
                                                       op0=ALU.mult), [tk], inc="V")
            t_call = O("vector", lambda e, tc=tc: e.scalar_tensor_tensor(
                c_all[:, 8 * tc:8 * (tc + 1)], g8_d[:, :], sm_c[:, 5:6], g8_a[:, :],
                op0=ALU.mult, op1=ALU.add), [tk], inc="V")

        t_w1_free = [t_arw_free] * NKC
        t_w2_free = [t_wo_free[0]] * 16
        t_w1_dma = [None] * NKC
        t_w2_dma = [None] * 16
        t_yacc_free = [t_ofm_free] * 8
        t_rowb_free = None
        t_hsb_free = [t_et_free[1] or t_et_free[0], t_x2_done]
        t_hbf_free = [t_wo_free[1], t_wo_free[1]]
        t_hfm_free = [t_wo_free[1], t_wo_free[1]]
        t_out_st = []
        t_b2t = None
        t_cb2 = None

        for half in range(2):
            tcs = list(range(8 * half, 8 * half + 8))
            t_cta = []
            for i, tc in enumerate(tcs):
                tmm = O("tensor", lambda e, tc=tc: e.transpose(
                    PD[0:8, 0:128], c_all[:, 8 * tc:8 * (tc + 1)], idf[:, :]),
                    [t_call, pd_free[0], t_rowb_free], inc="T")
                tev = O("vector", lambda e, i=i: e.tensor_copy(
                    cta[0:8, 128 * i:128 * (i + 1)], PD[0:8, 0:128]), [tmm], inc="V")
                pd_free[0] = tev
                t_cta.append(tev)
            t_b2t = DR(lambda e: e.dma_start(out=b2t[:, :], in_=b2_h[:, :]),
                       [t_cb2, t_rowb_free])
            for i, tc in enumerate(tcs):
                tl = DL(lambda e, tc=tc: e.dma_start(
                    out=xst[:, :], in_=x2_h[128 * tc:128 * (tc + 1), :]),
                    [t_xst_free, t_hsb_free[1]])
                tv = O("vector", lambda e, i=i, tc=tc: e.tensor_scalar(
                    yacc[i][:, :], xst[:, :], mu2[:, tc:tc + 1], rs2[:, tc:tc + 1],
                    op0=ALU.subtract, op1=ALU.mult), [tl, t_yacc_free[i]], inc="V")
                t_xst_free = tv
                tmm = None
                for g in range(2):
                    tmm = O("tensor", lambda e, i=i, g=g: e.matmul(
                        PC[:, 512 * g:512 * (g + 1)], cta[0:8, 128 * i:128 * (i + 1)],
                        b2t[:, 512 * g:512 * (g + 1)], start=True, stop=True),
                        [t_cta[i], t_b2t, pc_free[g]], inc="T" if g == 1 else None)
                tv = O("vector", lambda e, i=i: e.tensor_tensor(
                    yacc[i][:, :], yacc[i][:, :], PC[:, :], op=ALU.add), [tmm, tv], inc="V")
                pc_free[0] = pc_free[1] = tv
                t_yacc_free[i] = tv
                t_cb2 = tmm

            for e_ in range(ne_dbg):
                for kc in range(NKC):
                    O("gpsimd", lambda e, e_=e_, kc=kc: e.dma_start(
                        out=w1e[kc][:, :], in_=w1_h[e_, 128 * kc:128 * (kc + 1), :]),
                        [t_w1_free[kc]], inc="dW1", amt=16)
                t_w1_grp = ("dW1", kb.cnt["dW1"])
                for kc in range(NKC):
                    t_w1_dma[kc] = t_w1_grp
                for k in range(NKC):
                    O("gpsimd", lambda e, e_=e_, k=k: e.dma_start(
                        out=arw[k][:, 2048:4096].rearrange("p (two dd) -> p two dd", two=2),
                        in_=w2_h[e_, 256 * k:256 * (k + 1), :].rearrange(
                            "(two p) dd -> p two dd", p=128)),
                        [t_w2_free[2 * k], t_w2_free[2 * k + 1]], inc="dW2", amt=16)
                t_w2_grp = ("dW2", kb.cnt["dW2"])
                for k in range(16):
                    t_w2_dma[k] = t_w2_grp
                t_repl = []
                tprev = [t_cb2, t_rowb_free, t_ofm_free]
                for src_h, dst in ((b1_h, b1r), (l1g_h, gr_), (l1b_h, brr)):
                    tl = DR(lambda e, src_h=src_h, e_=e_: e.dma_start(
                        out=rowb[:, :], in_=src_h[e_:e_ + 1, :]), tprev)
                    tr = repl_rows(dst, rowb, HF, wait=[tl])
                    tprev = [tr]
                    t_repl.append(tr)
                t_rowb_free = t_repl[-1]

                def emit_w1_and_chain(i, tc, t_repl=t_repl):
                    global_state = None
                    pp = i % 2
                    tmm = None
                    for kc in range(NKC):
                        for g in range(4):
                            pq = (PA, PB)[g // 2][:, 512 * (g % 2):512 * (g % 2 + 1)]
                            fr = pa_free[g % 2] if g < 2 else pb_free[g % 2]
                            tmm = O("tensor", lambda e, kc=kc, g=g, tc=tc, pq=pq: e.matmul(
                                pq, lnar[kc][:, 128 * tc:128 * (tc + 1)],
                                w1e[kc][:, 512 * g:512 * (g + 1)],
                                start=(kc == 0), stop=(kc == NKC - 1)),
                                [t_w1_dma[kc], t_lnar2, fr],
                                inc="T" if (kc == NKC - 1 and g == 3) else None)
                    hx = hsb[pp]
                    tv = O("vector", lambda e, hx=hx: e.tensor_tensor(
                        hx[:, 0:1024], PA[:, :], b1r[:, 0:1024], op=ALU.add),
                        [tmm, t_repl[0], t_hsb_free[pp]], inc="V")
                    pa_f = tv
                    tv = O("vector", lambda e, hx=hx: e.tensor_tensor(
                        hx[:, 1024:2048], PB[:, :], b1r[:, 1024:2048], op=ALU.add), [tv], inc="V")
                    pb_f = tv
                    return tmm, tv, pa_f, pb_f

                def emit_chain(i, tc, e_, tv, t_repl=t_repl):
                    pp = i % 2
                    hx = hsb[pp]
                    ts_ = ln_stats(hx, HF, sm_c[:, 6:7], sm_c[:, 7:8], wait=[tv])
                    tv = O("vector", lambda e, hx=hx: e.tensor_scalar(
                        hx[:, :], hx[:, :], sm_c[:, 6:7], sm_c[:, 7:8],
                        op0=ALU.subtract, op1=ALU.mult), [ts_], inc="V")
                    tv = O("vector", lambda e, hx=hx: e.tensor_tensor(
                        hx[:, :], hx[:, :], gr_, op=ALU.mult), [tv, t_repl[1]], inc="V")
                    tv = O("vector", lambda e, hx=hx: e.tensor_tensor(
                        hx[:, :], hx[:, :], brr, op=ALU.add), [tv, t_repl[2]], inc="V")
                    tg = O("scalar", lambda e, hx=hx, pp=pp: e.activation(
                        hbf[pp][:, :], hx[:, :], AF.Gelu, bias=0.0, scale=1.0),
                        [tv, t_hbf_free[pp]], inc="A")
                    t_hsb_free[pp] = tg
                    tv = O("vector", lambda e, pp=pp, tc=tc, e_=e_: e.tensor_scalar(
                        hbf[pp][:, :], hbf[pp][:, :], c_all[:, 8 * tc + e_:8 * tc + e_ + 1], None,
                        op0=ALU.mult), [tg, t_call], inc="V")
                    return tv

                def emit_tail(i, tc, tv_c):
                    nonlocal t_w2_last
                    pp = i % 2
                    tev = None
                    for g in range(2):
                        tmm = None
                        for j in range(8):
                            tmm = O("tensor", lambda e, g=g, j=j, pp=pp: e.transpose(
                                PT[:, 128 * j:128 * (j + 1)],
                                hbf[pp][:, 1024 * g + 128 * j:1024 * g + 128 * (j + 1)], idb[:, :]),
                                [tv_c, pt_free[0]], inc="T" if j == 7 else None)
                        tev = O("vector", lambda e, g=g, pp=pp: e.tensor_copy(
                            hfm[pp][:, 1024 * g:1024 * (g + 1)], PT[:, :]),
                            [tmm, t_hfm_free[pp]], inc="V")
                        pt_free[0] = tev
                    t_hbf_free[pp] = tmm
                    tmm = None
                    for k in range(16):
                        for g in range(2):
                            tmm = O("tensor", lambda e, k=k, g=g, pp=pp: e.matmul(
                                PC[:, 512 * g:512 * (g + 1)],
                                hfm[pp][:, 128 * k:128 * (k + 1)],
                                w2e[k][:, 512 * g:512 * (g + 1)],
                                start=(k == 0), stop=(k == 15)),
                                [tev, t_w2_dma[k], pc_free[g]],
                                inc="T" if (k == 15 and g == 1) else None)
                    t_hfm_free[pp] = tmm
                    t_w2_last = tmm
                    return tmm, pp

                def emit_yevac(i, tmm):
                    tv = O("vector", lambda e, i=i: e.tensor_tensor(
                        yacc[i][:, :], yacc[i][:, :], PC[:, :], op=ALU.add),
                        [tmm, t_yacc_free[i]], inc="V")
                    pc_free[0] = pc_free[1] = tv
                    t_yacc_free[i] = tv

                t_w2_last = None
                pend = None
                for i, tc in enumerate(tcs):
                    tmm1, tvh, pa_f, pb_f = emit_w1_and_chain(i, tc)
                    t_w1_last = tmm1
                    if pend is not None:
                        j, tvc_j = pend
                        tmm2, _ = emit_tail(j, tcs[j], tvc_j)
                    tv_c = emit_chain(i, tc, e_, tvh)
                    pa_free[0] = pa_free[1] = pa_f
                    pb_free[0] = pb_free[1] = pb_f
                    if pend is not None:
                        j, _ = pend
                        emit_yevac(j, tmm2)
                    pend = (i, tv_c)
                j, tvc_j = pend
                tmm2, _ = emit_tail(j, tcs[j], tvc_j)
                emit_yevac(j, tmm2)
                t_w1_free = [t_w1_last] * NKC
                t_w2_free = [t_w2_last] * 16

            if dbg == 2:
                continue
            tl = DR(lambda e: e.dma_start(out=rowb[:, 0:1024], in_=lng_h[:, :]),
                    [t_rowb_free])
            tg1 = repl_rows(lngr, rowb[:, 0:1024], 1024,
                            wait=[tl, t_hfm_free[0], t_hfm_free[1]])
            tl = DR(lambda e: e.dma_start(out=rowb[:, 1024:2048], in_=lnb_h[:, :]))
            tg2 = repl_rows(lnbr, rowb[:, 1024:2048], 1024, wait=[tl])
            t_rowb_free = tg2
            for i, tc in enumerate(tcs):
                pp = i % 2
                tl = DL(lambda e, tc=tc: e.dma_start(
                    out=xst[:, :], in_=x2_h[128 * tc:128 * (tc + 1), :]),
                    [t_xst_free, t_hsb_free[0], t_hsb_free[1]])
                ts_ = ln_stats(yacc[i], D, sm_c[:, 6:7], sm_c[:, 7:8], wait=[t_yacc_free[i]])
                tv = O("vector", lambda e, i=i, pp=pp: e.tensor_scalar(
                    outsc[pp][:, :], yacc[i][:, :], sm_c[:, 6:7], sm_c[:, 7:8],
                    op0=ALU.subtract, op1=ALU.mult),
                    [ts_, t_hbf_free[pp]], inc="V")
                tv = O("vector", lambda e, pp=pp: e.tensor_tensor(
                    outsc[pp][:, :], outsc[pp][:, :], lngr, op=ALU.mult), [tv, tg1], inc="V")
                tv = O("vector", lambda e, pp=pp: e.tensor_tensor(
                    outsc[pp][:, :], outsc[pp][:, :], lnbr, op=ALU.add), [tv, tg2], inc="V")
                tv = O("vector", lambda e, pp=pp: e.tensor_tensor(
                    outsc[pp][:, :], outsc[pp][:, :], xst[:, :], op=ALU.add), [tv, tl], inc="V")
                t_xst_free = tv
                tst = DX(lambda e, tc=tc, pp=pp: e.dma_start(
                    out=out_h[128 * tc:128 * (tc + 1), :], in_=outsc[pp][:, :]), [tv])
                t_out_st.append(tst)
                t_hbf_free[pp] = tst

        DX(lambda e: e.dma_start(out=x2o_h[:, :], in_=x2_h[:, :]), [t_x2_done])
        if dbg == 2:
            t_out_st = [t_yacc_free[7]]
            for j in range(8):
                DX(lambda e, j=j: e.dma_start(out=dbg_h["d_yacc"][:, 1024 * j:1024 * (j + 1)],
                                              in_=yacc[j][:, :]), [t_out_st[-1]])
            DX(lambda e: e.dma_start(out=dbg_h["d_lngr"][:, :], in_=lngr[:, :]), [t_out_st[-1]])
            DX(lambda e: e.dma_start(out=dbg_h["d_lnbr"][:, :], in_=lnbr[:, :]), [t_out_st[-1]])
            DX(lambda e: e.dma_start(out=dbg_h["d_b1r"][:, :], in_=b1r[:, :]), [t_out_st[-1]])
            DX(lambda e: e.dma_start(out=dbg_h["d_hsb0"][:, :], in_=hsb[0][:, :]), [t_out_st[-1]])
            DX(lambda e: e.dma_start(out=dbg_h["d_hbf0"][:, :], in_=hbf[0][:, :]), [t_out_st[-1]])
            DX(lambda e: e.dma_start(out=dbg_h["d_hfm0"][:, :], in_=hfm[0][:, :]), [t_out_st[-1]])
            for k in range(16):
                DX(lambda e, k=k: e.dma_start(out=dbg_h["d_w2e"][:, 1024 * k:1024 * (k + 1)],
                                              in_=w2e[k][:, :]), [t_out_st[-1]])
            for k in range(8):
                DX(lambda e, k=k: e.dma_start(out=dbg_h["d_w1e"][:, 2048 * k:2048 * (k + 1)],
                                              in_=w1e[k][:, :]), [t_out_st[-1]])
            DX(lambda e: e.dma_start(out=dbg_h["d_gr"][:, :], in_=gr_[:, :]), [t_out_st[-1]])
            DX(lambda e: e.dma_start(out=dbg_h["d_brr"][:, :], in_=brr[:, :]), [t_out_st[-1]])
            DX(lambda e: e.dma_start(out=dbg_h["d_call"][:, :], in_=c_all[:, :]), [t_out_st[-1]])
            DX(lambda e: e.dma_start(out=dbg_h["d_mu2"][:, :], in_=mu2[:, :]), [t_out_st[-1]])
            DX(lambda e: e.dma_start(out=dbg_h["d_rs2"][:, :], in_=rs2[:, :]), [t_out_st[-1]])
            for k in range(NKC):
                DX(lambda e, k=k: e.dma_start(out=dbg_h["d_ln2"][128 * k:128 * (k + 1), :],
                                              in_=lnar[k][:, :]), [t_out_st[-1]])
        kb.wait("sync", [("dX", kb.cnt["dX"]), ("dL", kb.cnt["dL"]),
                         ("dR", kb.cnt["dR"]), ("dT", kb.cnt["dT"])])
        kb.wait("gpsimd", [("dW", kb.cnt["dW"]), ("dW1", kb.cnt["dW1"]),
                           ("dW2", kb.cnt["dW2"])])
        kb.emit_all(block)
    return nc


_CACHE = {}


def kernel(**inputs):
    if "nc" not in _CACHE:
        _CACHE["nc"] = build_nc()
    nc = _CACHE["nc"]
    f32 = np.float32
    x = np.ascontiguousarray(inputs["L0_attn_x"], dtype=f32)
    base = dict(
        wqkv=np.ascontiguousarray(inputs["L0_attn_Wqkv"][0:2], dtype=f32),
        bqkv=np.ascontiguousarray(inputs["L0_attn_bqkv"][0:2], dtype=f32),
        wout=np.ascontiguousarray(inputs["L0_attn_Wout"][0:2], dtype=f32),
        bout=np.ascontiguousarray(inputs["L0_attn_bout"][0:2], dtype=f32),
        gate_w=np.ascontiguousarray(inputs["L0_ffn_gate_w"], dtype=f32),
        w1=np.ascontiguousarray(inputs["L0_ffn_w1"], dtype=f32),
        b1=np.ascontiguousarray(inputs["L0_ffn_b1"], dtype=f32),
        ln1g=np.ascontiguousarray(inputs["L0_ffn_ln1_g"], dtype=f32),
        ln1b=np.ascontiguousarray(inputs["L0_ffn_ln1_b"], dtype=f32),
        w2=np.ascontiguousarray(inputs["L0_ffn_w2"], dtype=f32),
        b2=np.ascontiguousarray(inputs["L0_ffn_b2"], dtype=f32),
        lng=np.ascontiguousarray(inputs["L0_ffn_ln_g"], dtype=f32).reshape(1, D),
        lnb=np.ascontiguousarray(inputs["L0_ffn_ln_b"], dtype=f32).reshape(1, D),
        ident=np.eye(P, dtype=f32),
    )
    in_maps = [dict(base, x=np.ascontiguousarray(x[b])) for b in range(8)]
    res = run_bass_kernel_spmd(nc, in_maps, core_ids=list(range(8)))
    out = np.stack([np.asarray(r["out"]) for r in res.results], axis=0)
    return out.astype(np.float32)


# version 03
# speedup vs baseline: 1.3180x; 1.3180x over previous
import numpy as np
import ml_dtypes
import concourse.bass as bass
import concourse.mybir as mybir
from concourse.bass_utils import run_bass_kernel_spmd

F32 = mybir.dt.float32
F32R = mybir.dt.float32r
BF16 = mybir.dt.bfloat16
I32 = mybir.dt.int32
AF = mybir.ActivationFunctionType
ALU = mybir.AluOpType
AX = mybir.AxisListType

P = 128
T = 2048
D = 1024
NH = 16
SEG = 512
NTC = 16
NKC = 8
NE = 8
HF = 2048
EPS = 1e-5
MAGIC = 0x5F3759DF
ENGINES = ("sync", "gpsimd", "tensor", "vector", "scalar")


class KB:
    def __init__(self, nc):
        self.nc = nc
        self.q = {e: [] for e in ENGINES}
        self.cnt = {}
        self.sems = {}

    def new_sem(self, stack, name):
        self.sems[name] = stack.enter_context(self.nc.semaphore(name))
        self.cnt[name] = 0

    def op(self, eng, fn, waits=(), inc=None, amt=1):
        ws = tuple((s, v) for w in waits if w is not None
                   for (s, v) in [w] if v > 0)

        def emit(engine, ws=ws, fn=fn, inc=inc, amt=amt):
            for s, v in ws:
                engine.wait_ge(self.sems[s], v)
            inst = fn(engine)
            if inc is not None:
                inst.then_inc(self.sems[inc], amt)
        self.q[eng].append(emit)
        if inc is not None:
            self.cnt[inc] += amt
            return (inc, self.cnt[inc])
        return None

    def wait(self, eng, waits):
        ws = tuple((s, v) for w in waits if w is not None
                   for (s, v) in [w] if v > 0)
        if ws:
            def emit(engine, ws=ws):
                for s, v in ws:
                    engine.wait_ge(self.sems[s], v)
            self.q[eng].append(emit)

    def emit_all(self, block):
        for name in ENGINES:
            def body(engine, name=name):
                for emit in self.q[name]:
                    emit(engine)
            getattr(block, name)(body)


def build_nc(dbg=False, ne_dbg=NE):
    nc = bass.Bass("TRN2", target_bir_lowering=False)
    x_h = nc.dram_tensor("x", [T, D], F32, kind="ExternalInput").ap()
    wqkv_h = nc.dram_tensor("wqkv", [2, 3 * D, D], F32, kind="ExternalInput").ap()
    bqkv_h = nc.dram_tensor("bqkv", [2, 3 * D], F32, kind="ExternalInput").ap()
    wout_h = nc.dram_tensor("wout", [2, D, D], F32, kind="ExternalInput").ap()
    bout_h = nc.dram_tensor("bout", [2, D], F32, kind="ExternalInput").ap()
    gw_h = nc.dram_tensor("gate_w", [D, NE], F32, kind="ExternalInput").ap()
    w1_h = nc.dram_tensor("w1", [NE, D, HF], F32, kind="ExternalInput").ap()
    b1_h = nc.dram_tensor("b1", [NE, HF], F32, kind="ExternalInput").ap()
    l1g_h = nc.dram_tensor("ln1g", [NE, HF], F32, kind="ExternalInput").ap()
    l1b_h = nc.dram_tensor("ln1b", [NE, HF], F32, kind="ExternalInput").ap()
    w2_h = nc.dram_tensor("w2", [NE, HF, D], F32, kind="ExternalInput").ap()
    b2_h = nc.dram_tensor("b2", [NE, D], F32, kind="ExternalInput").ap()
    lng_h = nc.dram_tensor("lng", [1, D], F32, kind="ExternalInput").ap()
    lnb_h = nc.dram_tensor("lnb", [1, D], F32, kind="ExternalInput").ap()
    id_h = nc.dram_tensor("ident", [P, P], F32, kind="ExternalInput").ap()
    out_h = nc.dram_tensor("out", [T, D], F32, kind="ExternalOutput").ap()
    x2o_h = nc.dram_tensor("x2dbg", [T, D], F32, kind="ExternalOutput").ap()
    dbg_h = {}
    if dbg == 2:
        for nm, shape, dt in [("d_call", [P, P], F32), ("d_ln2", [D, T], BF16),
                              ("d_mu2", [P, NTC], F32), ("d_rs2", [P, NTC], F32),
                              ("d_yacc", [P, 8192], F32), ("d_lngr", [P, D], F32),
                              ("d_lnbr", [P, D], F32), ("d_b1r", [P, HF], F32),
                              ("d_gr", [P, HF], F32), ("d_brr", [P, HF], F32),
                              ("d_hsb0", [P, HF], F32), ("d_hbf0", [P, HF], BF16),
                              ("d_hfm0", [P, HF], BF16), ("d_w2e", [P, 16 * 1024], BF16),
                              ("d_w1e", [P, 16 * 1024], BF16)]:
            dbg_h[nm] = nc.dram_tensor(nm, shape, dt, kind="ExternalOutput").ap()
    if dbg == 1:
        for nm, shape, dt in [("d_lnar", [D, T], BF16), ("d_qf", [P, 4096], BF16),
                              ("d_kf", [P, 4096], BF16), ("d_vt", [P, 4096], BF16),
                              ("d_otm", [P, 4096], BF16), ("d_ofm0", [D, D], BF16),
                              ("d_ofm1", [D, D], BF16), ("d_zt", [P, 16], F32),
                              ("d_wo0", [D, D], BF16), ("d_wo1", [D, D], BF16)]:
            dbg_h[nm] = nc.dram_tensor(nm, shape, dt, kind="ExternalOutput").ap()
    wqkvb_h = nc.dram_tensor("wqkvb", [2, 3 * D, D], BF16, kind="Internal").ap()
    woutb_h = nc.dram_tensor("woutb", [2, D, D], BF16, kind="Internal").ap()
    x2_h = nc.dram_tensor("x2scr", [T, D], F32, kind="Internal").ap()

    xp_h = x_h.rearrange("(c j par) d -> c par j d", j=64, par=2)
    x2p_h = x2_h.rearrange("(c j par) d -> c par j d", j=64, par=2)

    from contextlib import ExitStack
    with ExitStack() as stack:
        def sbt(name, shape, dt):
            return stack.enter_context(nc.sbuf_tensor(name, shape, dt))

        def pstt(name, shape, dt):
            return stack.enter_context(nc.psum_tensor(name, shape, dt))

        idb = sbt("idb", [P, P], BF16)
        idf = sbt("idf", [P, P], F32)
        ones_r = sbt("ones_r", [1, P], F32)
        ones_rb = sbt("ones_rb", [1, P], BF16)
        rowbb = sbt("rowbb", [1, HF], BF16)
        onec = sbt("onec", [P, 1], F32)
        bstg = sbt("bstg", [24, P], F32)
        bq = sbt("bq", [P, 48], F32)
        c_all = sbt("c_all", [P, P], F32)
        mu2 = sbt("mu2", [P, NTC], F32)
        rs2 = sbt("rs2", [P, NTC], F32)
        gwt = sbt("gwt", [P, 64], F32)
        csr = sbt("csr", [P, NE], F32)
        cs_row = sbt("cs_row", [1, NE], F32)
        U8 = sbt("U8", [8, 2048], F32)
        sm_a = sbt("sm_a", [P, 16], F32)
        sm_b = sbt("sm_b", [P, 16], F32)
        sm_c = sbt("sm_c", [P, 16], F32)
        sm_i = sbt("sm_i", [P, 16], I32)
        bn_raw = sbt("bn_raw", [P, 24], F32)
        bn_ag = sbt("bn_ag", [P, 2], F32)
        g8_a = sbt("g8_a", [P, NE], F32)
        g8_b = sbt("g8_b", [P, NE], F32)
        g8_d = sbt("g8_d", [P, NE], F32)
        zt = sbt("zt", [P, 16], F32)
        lnar = [sbt(f"lnar{k}", [P, T], BF16) for k in range(NKC)]
        arw = [sbt(f"arw{k}", [P, 4096], BF16) for k in range(NKC)]
        arx = [sbt(f"arx{k}", [P, 4096], BF16) for k in range(12)]

        wqkT = [arw[k][:, 0:3072] for k in range(NKC)]
        woT = [[arw[k][:, 3072:4096] for k in range(NKC)],
               [arx[10 + j // 4][:, (j % 4) * 1024:(j % 4 + 1) * 1024] for j in range(NKC)]]
        ofm = [[arx[0 + j // 4][:, (j % 4) * 1024:(j % 4 + 1) * 1024] for j in range(NKC)],
               [arx[2 + j // 4][:, (j % 4) * 1024:(j % 4 + 1) * 1024] for j in range(NKC)]]
        qf = arx[4]
        lntm = arx[4][:, 2048:3072]
        kf = arx[5]
        vt = arx[6]
        otm = arx[7]
        et = [arx[8][:, 0:2048], arx[8][:, 2048:4096]]
        bvr = arx[9][:, 0:2048].bitcast(F32)
        bo3 = bvr
        xst = arx[9][:, 2048:4096].bitcast(F32)
        wtmp = arx[7][:, 0:2048].bitcast(F32)
        yacc = [arx[j // 2][:, (j % 2) * 2048:(j % 2 + 1) * 2048].bitcast(F32) for j in range(8)]
        w1e = [arw[k][:, 0:2048] for k in range(NKC)]
        w2e = [arw[j // 2][:, 2048 + (j % 2) * 1024:2048 + (j % 2 + 1) * 1024] for j in range(16)]
        b1r = arx[4][:, :].bitcast(F32)
        gr_ = arx[5][:, :].bitcast(F32)
        brr = arx[6][:, :].bitcast(F32)
        x2t = arx[7][:, 0:2048].bitcast(F32)
        hsb = [arx[8][:, :].bitcast(F32), arx[9][:, :].bitcast(F32)]
        hbf = [arx[11][:, 0:2048], arx[11][:, 2048:4096]]
        hfm = [arx[10][:, 0:2048], arx[10][:, 2048:4096]]
        rowb = U8[0:1, 0:2048]
        cta = U8[:, 0:1024]
        b2t = U8[0:8, 1024:2048]
        lngr = arx[10][:, 0:2048].bitcast(F32)
        lnbr = arx[10][:, 2048:4096].bitcast(F32)
        outsc = [arx[11][:, 0:2048].bitcast(F32), arx[11][:, 2048:4096].bitcast(F32)]

        PA = pstt("PA", [P, 1024], F32)
        PB = pstt("PB", [P, 1024], F32)
        PC = pstt("PC", [P, 1024], F32)
        PD = pstt("PD", [P, 512], F32)
        PT = pstt("PT", [P, 1024], BF16)

        kb = KB(nc)
        for s in ("dX", "dW", "dW1", "dW2", "dL", "dR", "dT", "T", "V", "A"):
            kb.new_sem(stack, s)
        block = stack.enter_context(nc.Block())
        O = kb.op

        def DX(fn, waits=()):
            return O("sync", fn, waits, inc="dX", amt=16)

        def DW(fn, waits=()):
            return O("gpsimd", fn, waits, inc="dW", amt=16)

        def DL(fn, waits=()):
            return O("sync", fn, waits, inc="dL", amt=16)

        last_dr = [None]

        def DR(fn, waits=()):
            tk = O("sync", fn, list(waits) + [last_dr[0]], inc="dR", amt=16)
            last_dr[0] = tk
            return tk

        def DT(fn, waits=()):
            return O("sync", fn, waits, inc="dT", amt=16)

        pd_free = [None]

        def newton_rsqrt(var_ap, out_ap, n, wait=()):
            vpe = sm_a[:, 0:n]
            t2 = sm_b[:, 0:n]
            ii = sm_i[:, 0:n]
            tk = O("vector", lambda e: e.tensor_scalar(vpe, var_ap, EPS, None, op0=ALU.add), wait, inc="V")
            tk = O("vector", lambda e: e.tensor_scalar(ii, vpe.bitcast(I32), 1, None, op0=ALU.arith_shift_right), [tk], inc="V")
            tk = O("vector", lambda e: e.tensor_scalar(ii, ii, -1.0, float(MAGIC), op0=ALU.mult, op1=ALU.add), [tk], inc="V")
            tk = O("vector", lambda e: e.tensor_copy(out_ap, ii.bitcast(F32)), [tk], inc="V")
            for _ in range(2):
                tk = O("vector", lambda e: e.tensor_tensor(t2, out_ap, out_ap, op=ALU.mult), [tk], inc="V")
                tk = O("vector", lambda e: e.tensor_tensor(t2, t2, vpe, op=ALU.mult), [tk], inc="V")
                tk = O("vector", lambda e: e.tensor_scalar(t2, t2, -0.5, 1.5, op0=ALU.mult, op1=ALU.add), [tk], inc="V")
                tk = O("vector", lambda e: e.tensor_tensor(out_ap, out_ap, t2, op=ALU.mult), [tk], inc="V")
            return tk

        def ln_stats(src_ap, width, mu_ap, rstd_ap, wait=()):
            ng = width // 512
            tk = None
            for i in range(ng):
                tk = O("vector", lambda e, i=i: e.bn_stats(
                    bn_raw[:, 6 * i:6 * i + 6], src_ap[:, 512 * i:512 * (i + 1)]),
                    wait if i == 0 else [], inc="V")
            tk = O("vector", lambda e, ng=ng: e.bn_aggr(bn_ag[:, :], bn_raw[:, 0:6 * ng]), [tk], inc="V")
            tk = O("vector", lambda e: e.tensor_copy(mu_ap, bn_ag[:, 0:1]), [tk], inc="V")
            return newton_rsqrt(bn_ag[:, 1:2], rstd_ap, 1, wait=[tk])

        def repl_rows(dst_f32, row_ap, width, wait=()):
            rb = rowbb[:, 0:width]
            tc_ = O("vector", lambda e, rb=rb, row_ap=row_ap: e.tensor_copy(rb, row_ap),
                    list(wait), inc="V")
            tk_out = None
            for j in range(0, width, 512):
                w = min(512, width - j)
                tm = O("tensor", lambda e, j=j, w=w, rb=rb: e.matmul(
                    PD[:, 0:w], ones_rb[:, :], rb[:, j:j + w],
                    start=True, stop=True),
                    [tc_, pd_free[0]], inc="T")
                tk_out = O("vector", lambda e, j=j, w=w: e.tensor_copy(
                    dst_f32[:, j:j + w], PD[:, 0:w]), [tm], inc="V")
                pd_free[0] = tk_out
            return tk_out

        t_ones = O("vector", lambda e: e.memset(ones_r[:, :], 1.0), inc="V")
        O("vector", lambda e: e.memset(ones_rb[:, :], 1.0), inc="V")
        t_onec = O("vector", lambda e: e.memset(onec[:, :], 1.0), inc="V")
        t_id = DR(lambda e: e.dma_start(out=idf[:, :], in_=id_h[:, :]))
        t_idb = DW(lambda e: e.dma_start(out=idb[:, :], in_=id_h[:, :]))
        t_wqc = [DW(lambda e, b=b: e.dma_start(out=wqkvb_h[b], in_=wqkv_h[b])) for b in range(2)]
        t_woc = [DW(lambda e, b=b: e.dma_start(out=woutb_h[b], in_=wout_h[b])) for b in range(2)]
        t_bq = []
        tprev = None
        for br in range(2):
            tl = DR(lambda e, br=br: e.dma_start(
                out=bstg[:, :], in_=bqkv_h[br].rearrange("(c p) -> c p", p=128)),
                [tprev])
            tt_ = O("tensor", lambda e: e.transpose(PD[:, 0:24], bstg[:, :], idf[0:24, 0:24]),
                    [tl, t_id, pd_free[0]], inc="T")
            tk = O("vector", lambda e, br=br: e.tensor_copy(bq[:, 24 * br:24 * br + 24], PD[:, 0:24]),
                   [tt_], inc="V")
            pd_free[0] = tk
            tprev = tk
            t_bq.append(tk)
        t_gw = DR(lambda e: e.dma_start(
            out=gwt[:, :].rearrange("p (c e) -> p c e", e=NE),
            in_=gw_h.rearrange("(c p) e -> p c e", p=128)))
        tm = None
        for kc in range(NKC):
            tm = O("tensor", lambda e, kc=kc: e.matmul(
                PD[0:1, 0:NE], onec[:, :], gwt[:, 8 * kc:8 * kc + 8],
                start=(kc == 0), stop=(kc == NKC - 1)),
                [t_gw, t_onec, pd_free[0]], inc="T" if kc == NKC - 1 else None)
        t_csrow = O("vector", lambda e: e.tensor_copy(cs_row[:, :], PD[0:1, 0:NE]), [tm], inc="V")
        pd_free[0] = t_csrow
        tm = O("tensor", lambda e: e.matmul(PD[:, 0:NE], ones_r[:, :], cs_row[:, :],
                                            start=True, stop=True), [t_csrow, t_ones, pd_free[0]], inc="T")
        t_csr = O("vector", lambda e: e.tensor_copy(csr[:, :], PD[:, 0:NE]), [tm], inc="V")
        pd_free[0] = t_csr

        t_lnar = None
        t_arw_free = None
        t_ofm_free = None
        t_xst_free = None
        t_wtmp_free = None
        t_x2_done = None
        pa_free = [None, None]
        pb_free = [None, None]
        pc_free = [None, None]
        pt_free = [None]
        t_et_free = [None, None]
        t_otr_done = None
        t_wq_dma = [None] * NKC
        t_bvr = None
        t_woT_ready = None
        t_wo_free = [None, None]

        for span in range(2):
            base = span * 1024
            for i in range(8):
                tc = span * 8 + i
                tl = DL(lambda e, tc=tc: e.dma_start(
                    out=xst[:, :], in_=x_h[128 * tc:128 * (tc + 1), :]), [t_xst_free])
                ts_ = ln_stats(xst, D, sm_c[:, 0:1], sm_c[:, 1:2], wait=[tl])
                ta = O("vector", lambda e: e.tensor_scalar(
                    lntm[:, :], xst[:, :], sm_c[:, 0:1], sm_c[:, 1:2],
                    op0=ALU.subtract, op1=ALU.mult), [ts_], inc="V")
                t_xst_free = ta
                tmm = None
                for kc in range(NKC):
                    tmm = O("tensor", lambda e, kc=kc: e.transpose(
                        PT[:, 128 * kc:128 * (kc + 1)], lntm[:, 128 * kc:128 * (kc + 1)], idb[:, :]),
                        [ta, t_idb, pt_free[0]] if kc == 0 else [],
                        inc="T" if kc == NKC - 1 else None)
                tev = None
                for kc in range(NKC):
                    tev = O("vector", lambda e, kc=kc, tc=tc: e.tensor_copy(
                        lnar[kc][:, 128 * tc:128 * (tc + 1)], PT[:, 128 * kc:128 * (kc + 1)]),
                        [tmm], inc="V")
                pt_free[0] = tev
                t_lnar = tev

            for br in range(2):
                for kc in range(NKC):
                    t_wq_dma[kc] = DT(lambda e, kc=kc, br=br: e.dma_start(
                        out=wqkT[kc][:, :], in_=wqkvb_h[br][:, 128 * kc:128 * (kc + 1)],
                        transpose=True), [t_wqc[br], t_arw_free])
                if span == 0:
                    for kc in range(NKC):
                        t_woT_ready = DT(lambda e, kc=kc, br=br: e.dma_start(
                            out=woT[br][kc][:, :], in_=woutb_h[br][:, 128 * kc:128 * (kc + 1)],
                            transpose=True), [t_woc[br]])
                tl = DR(lambda e, br=br: e.dma_start(
                    out=rowb[:, 0:1024], in_=bqkv_h[br:br + 1, 2048:3072]), [t_bvr])
                t_bvr = repl_rows(bvr, rowb[:, 0:1024], 1024, wait=[tl])

                for seg in range(2):
                    if br == 0:
                        co, st_ = base + SEG * seg, 1
                    else:
                        co, st_ = base + seg, 2

                    t_qk_ready = None
                    for fc in range(16):
                        half = fc % 2
                        pa = PA[:, 512 * half:512 * (half + 1)]
                        tmm = None
                        for kc in range(NKC):
                            rhs_ap = lnar[kc][:, co:co + st_ * (SEG - 1) + 1:st_]
                            ws = ([t_wq_dma[NKC - 1], t_lnar, pa_free[half]] if kc == 0 else [])
                            tmm = O("tensor", lambda e, kc=kc, fc=fc, pa=pa, rhs_ap=rhs_ap: e.matmul(
                                pa, wqkT[kc][:, 128 * fc:128 * (fc + 1)], rhs_ap,
                                start=(kc == 0), stop=(kc == NKC - 1)),
                                ws,
                                inc="T" if kc == NKC - 1 else None)
                        dst = qf if fc < 8 else kf
                        dco = 512 * (fc % 8)
                        t_qk_ready = O("vector", lambda e, pa=pa, dst=dst, dco=dco, fc=fc, br=br: e.tensor_scalar(
                            dst[:, dco:dco + 512], pa, bq[:, 24 * br + fc:24 * br + fc + 1], None,
                            op0=ALU.add), [tmm, t_bq[br]], inc="V")
                        pa_free[half] = t_qk_ready
                    t_v_ready = None
                    for j in range(4):
                        tmm = None
                        for kc in range(NKC):
                            lhs_ap = lnar[kc][:, co + st_ * 128 * j:co + st_ * (128 * (j + 1) - 1) + 1:st_]
                            for g in range(2):
                                ws = ([t_wq_dma[NKC - 1], t_lnar, pb_free[g]] if kc == 0 else [])
                                tmm = O("tensor", lambda e, kc=kc, g=g, lhs_ap=lhs_ap: e.matmul(
                                    PB[:, 512 * g:512 * (g + 1)], lhs_ap,
                                    wqkT[kc][:, 2048 + 512 * g:2048 + 512 * (g + 1)],
                                    start=(kc == 0), stop=(kc == NKC - 1)),
                                    ws,
                                    inc="T" if (kc == NKC - 1 and g == 1) else None)
                        t_v_ready = O("vector", lambda e, j=j: e.tensor_tensor(
                            vt[:, 1024 * j:1024 * (j + 1)], PB[:, :], bvr, op=ALU.add),
                            [tmm, t_bvr], inc="V")
                        pb_free[0] = pb_free[1] = t_v_ready
                    t_arw_free = tmm

                    def emit_scores(h, t_qk_ready=t_qk_ready):
                        pi = 8 * (h % 2)
                        hp = 64 * (h % 2)
                        qh = qf[hp:hp + 64, 512 * (h // 2):512 * (h // 2) + 512]
                        kh = kf[hp:hp + 64, 512 * (h // 2):512 * (h // 2) + 512]
                        tS = []
                        for qc in range(4):
                            half = qc % 2
                            pa = PA[:, 512 * half:512 * (half + 1)]
                            tm1 = O("tensor", lambda e, qh=qh, kh=kh, qc=qc, pa=pa: e.matmul(
                                pa, qh[:, 128 * qc:128 * (qc + 1)], kh, start=True, stop=True),
                                [t_qk_ready, pa_free[half]], inc="T")
                            te = O("scalar", lambda e, pa=pa, qc=qc, pi=pi: e.activation(
                                pa, pa, AF.Exp, bias=0.0, scale=0.125,
                                accum_out=zt[:, pi + qc:pi + qc + 1]), [tm1], inc="A")
                            pa_free[half] = te
                            tS.append(te)
                        tc_ = []
                        for qc in range(4):
                            t1 = O("vector", lambda e, qc=qc, pi=pi: e.tensor_scalar(
                                zt[:, pi + qc:pi + qc + 1], zt[:, pi + qc:pi + qc + 1], 3.0, None,
                                op0=ALU.mult), [tS[qc]], inc="V")
                            t2 = O("vector", lambda e, qc=qc, pi=pi: e.reciprocal(
                                zt[:, pi + 4 + qc:pi + 5 + qc], zt[:, pi + qc:pi + qc + 1]),
                                [t1], inc="V")
                            tc_.append(t2)
                        eb = et[h % 2]
                        tE = None
                        for kc in range(4):
                            half = kc % 2
                            pb = PB[:, 512 * half:512 * (half + 1)]
                            tm1 = O("tensor", lambda e, qh=qh, kh=kh, kc=kc, pb=pb: e.matmul(
                                pb, kh[:, 128 * kc:128 * (kc + 1)], qh, start=True, stop=True),
                                [t_qk_ready, pb_free[half], t_et_free[h % 2]], inc="T")
                            tE = O("scalar", lambda e, pb=pb, eb=eb, kc=kc: e.activation(
                                eb[:, 512 * kc:512 * (kc + 1)], pb, AF.Exp, bias=0.0, scale=0.125),
                                [tm1], inc="A")
                            pb_free[half] = tE
                        return (h, tc_, tE)

                    def emit_pv(state, t_v_ready=t_v_ready):
                        h, tc_, tE = state
                        pi = 8 * (h % 2)
                        eb = et[h % 2]
                        last_ev = None
                        tmm = None
                        for qc in range(4):
                            half = qc % 2
                            pcs = PC[:, 512 * half:512 * half + 64]
                            tmm = None
                            for kc in range(4):
                                ws = ([tE, t_v_ready, pc_free[half]] if kc == 0 else [])
                                tmm = O("tensor", lambda e, qc=qc, kc=kc, h=h, eb=eb, pcs=pcs: e.matmul(
                                    pcs, eb[:, 512 * kc + 128 * qc:512 * kc + 128 * (qc + 1)],
                                    vt[:, 1024 * kc + 64 * h:1024 * kc + 64 * (h + 1)],
                                    start=(kc == 0), stop=(kc == 3)),
                                    ws, inc="T" if kc == 3 else None)
                            last_ev = O("vector", lambda e, qc=qc, h=h, pcs=pcs, pi=pi: e.tensor_scalar(
                                otm[:, 1024 * qc + 64 * h:1024 * qc + 64 * (h + 1)], pcs,
                                zt[:, pi + 4 + qc:pi + 5 + qc], None, op0=ALU.mult),
                                [tmm, tc_[qc], t_otr_done], inc="V")
                            pc_free[half] = last_ev
                        t_et_free[h % 2] = tmm
                        return last_ev

                    prev = None
                    t_seg_oevac = None
                    for h in range(NH):
                        st8 = emit_scores(h)
                        if prev is not None:
                            t_seg_oevac = emit_pv(prev)
                        prev = st8
                    t_seg_oevac = emit_pv(prev)

                    for fc in range(NKC):
                        tmm = None
                        for qc in range(4):
                            tmm = O("tensor", lambda e, fc=fc, qc=qc: e.transpose(
                                PT[:, 128 * qc:128 * (qc + 1)],
                                otm[:, 1024 * qc + 128 * fc:1024 * qc + 128 * (fc + 1)], idb[:, :]),
                                [t_seg_oevac, pt_free[0]] if qc == 0 else [],
                                inc="T" if qc == 3 else None)
                        if br == 0:
                            dst = ofm[0][fc][:, 512 * seg:512 * (seg + 1)].rearrange(
                                "p (b par j) -> p b j par", b=4, par=2, j=64)
                        else:
                            dst = ofm[1][fc].rearrange(
                                "p (jhi par jlo) -> p jhi par jlo", jhi=8, par=2, jlo=64)[:, :, seg, :]
                        tev = O("vector", lambda e, dst=dst: e.tensor_copy(
                            dst, PT[:, 0:512]), [tmm], inc="V")
                        pt_free[0] = tev
                    t_otr_done = tev

            tl0 = DR(lambda e: e.dma_start(out=rowb[:, 0:1024], in_=bout_h[0:1, :]), [t_bvr])
            tl1 = DR(lambda e: e.dma_start(out=rowb[:, 1024:2048], in_=bout_h[1:2, :]))
            tr0 = O("vector", lambda e: e.tensor_tensor(
                rowb[:, 0:1024], rowb[:, 0:1024], rowb[:, 1024:2048], op=ALU.add),
                [tl0, tl1], inc="V")
            tr0 = O("vector", lambda e: e.tensor_scalar(
                rowb[:, 0:1024], rowb[:, 0:1024], 1.0 / 3.0, None, op0=ALU.mult), [tr0], inc="V")
            t_bo3 = repl_rows(bo3, rowb[:, 0:1024], 1024, wait=[tr0])
            t_bvr = t_bo3

            tm0 = tm1 = None
            for i in range(8):
                tc = span * 8 + i
                tl = DL(lambda e, tc=tc: e.dma_start(
                    out=xst[:, :], in_=xp_h[tc]),
                    [t_xst_free])
                for kc in range(NKC):
                    l0 = ofm[0][kc][:, 128 * i:128 * (i + 1)]
                    for g in range(2):
                        ws = ([t_otr_done, pa_free[g], t_woT_ready] if kc == 0 else [])
                        tm0 = O("tensor", lambda e, kc=kc, g=g, l0=l0: e.matmul(
                            PA[:, 512 * g:512 * (g + 1)], l0, woT[0][kc][:, 512 * g:512 * (g + 1)],
                            start=(kc == 0), stop=(kc == NKC - 1)),
                            ws,
                            inc="T" if (kc == NKC - 1 and g == 1) else None)
                for kc in range(NKC):
                    l1 = ofm[1][kc][:, 128 * i:128 * (i + 1)]
                    for g in range(2):
                        ws = ([t_otr_done, pb_free[g], t_woT_ready] if kc == 0 else [])
                        tm1 = O("tensor", lambda e, kc=kc, g=g, l1=l1: e.matmul(
                            PB[:, 512 * g:512 * (g + 1)], l1, woT[1][kc][:, 512 * g:512 * (g + 1)],
                            start=(kc == 0), stop=(kc == NKC - 1)),
                            ws,
                            inc="T" if (kc == NKC - 1 and g == 1) else None)
                tv = O("vector", lambda e: e.tensor_tensor(wtmp[:, :], PA[:, :], bo3, op=ALU.add),
                       [tm0, t_bo3, t_wtmp_free], inc="V")
                pa_free[0] = pa_free[1] = tv
                tv = O("vector", lambda e: e.tensor_tensor(wtmp[:, :], wtmp[:, :], PB[:, :], op=ALU.add),
                       [tm1, tv], inc="V")
                pb_free[0] = pb_free[1] = tv
                tv = O("vector", lambda e: e.tensor_tensor(wtmp[:, :], wtmp[:, :], xst[:, :], op=ALU.add),
                       [tl, tv], inc="V")
                t_xst_free = tv
                tst = DX(lambda e, tc=tc: e.dma_start(
                    out=x2p_h[tc], in_=wtmp[:, :]), [tv])
                t_wtmp_free = tst
                t_x2_done = tst
            t_ofm_free = tm1
            t_wo_free = [tm0, tm1]

        if dbg == 1:
            for k in range(NKC):
                DX(lambda e, k=k: e.dma_start(out=dbg_h["d_lnar"][128 * k:128 * (k + 1), :],
                                              in_=lnar[k][:, :]), [t_x2_done])
            DX(lambda e: e.dma_start(out=dbg_h["d_qf"][:, :], in_=qf[:, :]), [t_x2_done])
            DX(lambda e: e.dma_start(out=dbg_h["d_kf"][:, :], in_=kf[:, :]), [t_x2_done])
            DX(lambda e: e.dma_start(out=dbg_h["d_vt"][:, :], in_=vt[:, :]), [t_x2_done])
            DX(lambda e: e.dma_start(out=dbg_h["d_otm"][:, :], in_=otm[:, :]), [t_x2_done])
            for k in range(NKC):
                DX(lambda e, k=k: e.dma_start(out=dbg_h["d_ofm0"][128 * k:128 * (k + 1), :],
                                              in_=ofm[0][k][:, :]), [t_x2_done])
                DX(lambda e, k=k: e.dma_start(out=dbg_h["d_ofm1"][128 * k:128 * (k + 1), :],
                                              in_=ofm[1][k][:, :]), [t_x2_done])
            DX(lambda e: e.dma_start(out=dbg_h["d_zt"][:, :], in_=zt[:, :]), [t_x2_done])
            for k in range(NKC):
                DX(lambda e, k=k: e.dma_start(out=dbg_h["d_wo0"][128 * k:128 * (k + 1), :],
                                              in_=woT[0][k][:, :]), [t_x2_done])
                DX(lambda e, k=k: e.dma_start(out=dbg_h["d_wo1"][128 * k:128 * (k + 1), :],
                                              in_=woT[1][k][:, :]), [t_x2_done])
            DX(lambda e: e.dma_start(out=x2o_h[:, :], in_=x2_h[:, :]), [t_x2_done])
            kb.wait("sync", [("dX", kb.cnt["dX"]), ("dL", kb.cnt["dL"]),
                             ("dR", kb.cnt["dR"]), ("dT", kb.cnt["dT"])])
            kb.wait("gpsimd", [("dW", kb.cnt["dW"])])
            kb.emit_all(block)
            return nc
        t_call = None
        t_lnar2 = None
        t_x2t_free = t_wtmp_free

        for tc in range(NTC):
            tl = DL(lambda e, tc=tc: e.dma_start(
                out=xst[:, :], in_=x2_h[128 * tc:128 * (tc + 1), :]),
                [t_xst_free, t_x2_done])
            ts_ = ln_stats(xst, D, mu2[:, tc:tc + 1], rs2[:, tc:tc + 1], wait=[tl])
            ta = O("vector", lambda e, tc=tc: e.tensor_scalar(
                lntm[:, :], xst[:, :], mu2[:, tc:tc + 1], rs2[:, tc:tc + 1],
                op0=ALU.subtract, op1=ALU.mult), [ts_], inc="V")
            tmm = None
            for kc in range(NKC):
                tmm = O("tensor", lambda e, kc=kc: e.transpose(
                    PT[:, 128 * kc:128 * (kc + 1)], lntm[:, 128 * kc:128 * (kc + 1)], idb[:, :]),
                    [ta, pt_free[0]] if kc == 0 else [],
                    inc="T" if kc == NKC - 1 else None)
            tev = None
            for kc in range(NKC):
                tev = O("vector", lambda e, kc=kc, tc=tc: e.tensor_copy(
                    lnar[kc][:, 128 * tc:128 * (tc + 1)], PT[:, 128 * kc:128 * (kc + 1)]),
                    [tmm], inc="V")
            pt_free[0] = tev
            t_lnar2 = tev
            t_x2t_ready = None
            for g in range(2):
                tmm = None
                for kc in range(4 * g, 4 * g + 4):
                    tmm = O("tensor", lambda e, kc=kc, g=g: e.transpose(
                        PC[:, 512 * g + 128 * (kc % 4):512 * g + 128 * (kc % 4 + 1)],
                        xst[:, 128 * kc:128 * (kc + 1)], idf[:, :]),
                        [ts_, pc_free[g], t_x2t_free] if kc % 4 == 0 else [],
                        inc="T" if kc % 4 == 3 else None)
                t_x2t_ready = O("vector", lambda e, g=g: e.tensor_copy(
                    x2t[:, 512 * g:512 * (g + 1)], PC[:, 512 * g:512 * (g + 1)]), [tmm], inc="V")
                pc_free[g] = t_x2t_ready
            t_xst_free = tmm
            tmm = None
            for kc in range(NKC):
                tmm = O("tensor", lambda e, kc=kc: e.matmul(
                    PD[:, 0:NE], x2t[:, 128 * kc:128 * (kc + 1)], gwt[:, 8 * kc:8 * kc + 8],
                    start=(kc == 0), stop=(kc == NKC - 1)),
                    [t_x2t_ready, t_gw, pd_free[0]] if kc == 0 else [],
                    inc="T" if kc == NKC - 1 else None)
            t_x2t_free = tmm
            tk = O("vector", lambda e, tc=tc: e.tensor_scalar(
                g8_a[:, :], csr[:, :], mu2[:, tc:tc + 1], None, op0=ALU.mult),
                [t_csr, ts_], inc="V")
            tk = O("vector", lambda e: e.tensor_tensor(g8_b[:, :], PD[:, 0:NE], g8_a[:, :],
                                                       op=ALU.subtract), [tmm, tk], inc="V")
            pd_free[0] = tk
            tk = O("vector", lambda e, tc=tc: e.tensor_scalar(
                g8_b[:, :], g8_b[:, :], rs2[:, tc:tc + 1], None, op0=ALU.mult), [tk], inc="V")
            tk = O("vector", lambda e: e.reduce_max(sm_c[:, 0:1], g8_b[:, :], axis=AX.X), [tk], inc="V")
            tk = O("vector", lambda e: e.tensor_scalar(g8_a[:, :], g8_b[:, :], sm_c[:, 0:1], None,
                                                       op0=ALU.is_equal), [tk], inc="V")
            tk = O("vector", lambda e: e.scalar_tensor_tensor(
                g8_d[:, :], g8_a[:, :], -1e30, g8_b[:, :], op0=ALU.mult, op1=ALU.add), [tk], inc="V")
            tk = O("vector", lambda e: e.reduce_max(sm_c[:, 1:2], g8_d[:, :], axis=AX.X), [tk], inc="V")
            tk = O("vector", lambda e: e.tensor_tensor(sm_c[:, 2:3], sm_c[:, 0:1], sm_c[:, 1:2],
                                                       op=ALU.subtract), [tk], inc="V")
            tk = O("scalar", lambda e: e.activation(sm_c[:, 3:4], sm_c[:, 2:3], AF.Exp,
                                                    bias=0.0, scale=-1.0), [tk], inc="A")
            tk = O("vector", lambda e: e.tensor_scalar(sm_c[:, 3:4], sm_c[:, 3:4], 1.0, None,
                                                       op0=ALU.add), [tk], inc="V")
            tk = O("vector", lambda e: e.reciprocal(sm_c[:, 4:5], sm_c[:, 3:4]), [tk], inc="V")
            tk = O("vector", lambda e: e.tensor_scalar(sm_c[:, 5:6], sm_c[:, 4:5], -1.0, 1.0,
                                                       op0=ALU.mult, op1=ALU.add), [tk], inc="V")
            tk = O("vector", lambda e: e.tensor_scalar(g8_d[:, :], g8_b[:, :], sm_c[:, 1:2], None,
                                                       op0=ALU.is_equal), [tk], inc="V")
            tk = O("vector", lambda e: e.tensor_scalar(g8_a[:, :], g8_a[:, :], sm_c[:, 4:5], None,
                                                       op0=ALU.mult), [tk], inc="V")
            t_call = O("vector", lambda e, tc=tc: e.scalar_tensor_tensor(
                c_all[:, 8 * tc:8 * (tc + 1)], g8_d[:, :], sm_c[:, 5:6], g8_a[:, :],
                op0=ALU.mult, op1=ALU.add), [tk], inc="V")

        t_w1_free = [t_arw_free] * NKC
        t_w2_free = [t_wo_free[0]] * 16
        t_w1_dma = [None] * NKC
        t_w2_dma = [None] * 16
        t_yacc_free = [t_ofm_free] * 8
        t_rowb_free = None
        t_hsb_free = [t_et_free[1] or t_et_free[0], t_x2_done]
        t_hbf_free = [t_wo_free[1], t_wo_free[1]]
        t_hfm_free = [t_wo_free[1], t_wo_free[1]]
        t_out_st = []
        t_b2t = None
        t_cb2 = None

        for half in range(2):
            tcs = list(range(8 * half, 8 * half + 8))
            t_cta = []
            for i, tc in enumerate(tcs):
                tmm = O("tensor", lambda e, tc=tc: e.transpose(
                    PD[0:8, 0:128], c_all[:, 8 * tc:8 * (tc + 1)], idf[:, :]),
                    [t_call, pd_free[0], t_rowb_free], inc="T")
                tev = O("vector", lambda e, i=i: e.tensor_copy(
                    cta[0:8, 128 * i:128 * (i + 1)], PD[0:8, 0:128]), [tmm], inc="V")
                pd_free[0] = tev
                t_cta.append(tev)
            t_b2t = DR(lambda e: e.dma_start(out=b2t[:, :], in_=b2_h[:, :]),
                       [t_cb2, t_rowb_free])
            for i, tc in enumerate(tcs):
                tl = DL(lambda e, tc=tc: e.dma_start(
                    out=xst[:, :], in_=x2_h[128 * tc:128 * (tc + 1), :]),
                    [t_xst_free, t_hsb_free[1]])
                tv = O("vector", lambda e, i=i, tc=tc: e.tensor_scalar(
                    yacc[i][:, :], xst[:, :], mu2[:, tc:tc + 1], rs2[:, tc:tc + 1],
                    op0=ALU.subtract, op1=ALU.mult), [tl, t_yacc_free[i]], inc="V")
                t_xst_free = tv
                tmm = None
                for g in range(2):
                    tmm = O("tensor", lambda e, i=i, g=g: e.matmul(
                        PC[:, 512 * g:512 * (g + 1)], cta[0:8, 128 * i:128 * (i + 1)],
                        b2t[:, 512 * g:512 * (g + 1)], start=True, stop=True),
                        [t_cta[i], t_b2t, pc_free[g]], inc="T" if g == 1 else None)
                tv = O("vector", lambda e, i=i: e.tensor_tensor(
                    yacc[i][:, :], yacc[i][:, :], PC[:, :], op=ALU.add), [tmm, tv], inc="V")
                pc_free[0] = pc_free[1] = tv
                t_yacc_free[i] = tv
                t_cb2 = tmm

            for e_ in range(ne_dbg):
                for kc in range(NKC):
                    O("gpsimd", lambda e, e_=e_, kc=kc: e.dma_start(
                        out=w1e[kc][:, :], in_=w1_h[e_, 128 * kc:128 * (kc + 1), :]),
                        [t_w1_free[kc]], inc="dW1", amt=16)
                t_w1_grp = ("dW1", kb.cnt["dW1"])
                for kc in range(NKC):
                    t_w1_dma[kc] = t_w1_grp
                for k in range(NKC):
                    O("gpsimd", lambda e, e_=e_, k=k: e.dma_start(
                        out=arw[k][:, 2048:4096].rearrange("p (two dd) -> p two dd", two=2),
                        in_=w2_h[e_, 256 * k:256 * (k + 1), :].rearrange(
                            "(two p) dd -> p two dd", p=128)),
                        [t_w2_free[2 * k], t_w2_free[2 * k + 1]], inc="dW2", amt=16)
                t_w2_grp = ("dW2", kb.cnt["dW2"])
                for k in range(16):
                    t_w2_dma[k] = t_w2_grp
                t_repl = []
                tprev = [t_cb2, t_rowb_free, t_ofm_free]
                for src_h, dst in ((b1_h, b1r), (l1g_h, gr_), (l1b_h, brr)):
                    tl = DR(lambda e, src_h=src_h, e_=e_: e.dma_start(
                        out=rowb[:, :], in_=src_h[e_:e_ + 1, :]), tprev)
                    tr = repl_rows(dst, rowb, HF, wait=[tl])
                    tprev = [tr]
                    t_repl.append(tr)
                t_rowb_free = t_repl[-1]

                def emit_w1_and_chain(i, tc, t_repl=t_repl):
                    global_state = None
                    pp = i % 2
                    tmm = None
                    for kc in range(NKC):
                        for g in range(4):
                            pq = (PA, PB)[g // 2][:, 512 * (g % 2):512 * (g % 2 + 1)]
                            fr = pa_free[g % 2] if g < 2 else pb_free[g % 2]
                            ws = ([t_w1_dma[kc], t_lnar2, fr] if kc == 0 else [])
                            tmm = O("tensor", lambda e, kc=kc, g=g, tc=tc, pq=pq: e.matmul(
                                pq, lnar[kc][:, 128 * tc:128 * (tc + 1)],
                                w1e[kc][:, 512 * g:512 * (g + 1)],
                                start=(kc == 0), stop=(kc == NKC - 1)),
                                ws,
                                inc="T" if (kc == NKC - 1 and g == 3) else None)
                    hx = hsb[pp]
                    tv = O("vector", lambda e, hx=hx: e.tensor_tensor(
                        hx[:, 0:1024], PA[:, :], b1r[:, 0:1024], op=ALU.add),
                        [tmm, t_repl[0], t_hsb_free[pp]], inc="V")
                    pa_f = tv
                    tv = O("vector", lambda e, hx=hx: e.tensor_tensor(
                        hx[:, 1024:2048], PB[:, :], b1r[:, 1024:2048], op=ALU.add), [tv], inc="V")
                    pb_f = tv
                    return tmm, tv, pa_f, pb_f

                def emit_chain(i, tc, e_, tv, t_repl=t_repl):
                    pp = i % 2
                    hx = hsb[pp]
                    ts_ = ln_stats(hx, HF, sm_c[:, 6:7], sm_c[:, 7:8], wait=[tv])
                    tv = O("vector", lambda e, hx=hx: e.tensor_scalar(
                        hx[:, :], hx[:, :], sm_c[:, 6:7], sm_c[:, 7:8],
                        op0=ALU.subtract, op1=ALU.mult), [ts_], inc="V")
                    tv = O("vector", lambda e, hx=hx: e.tensor_tensor(
                        hx[:, :], hx[:, :], gr_, op=ALU.mult), [tv, t_repl[1]], inc="V")
                    tv = O("vector", lambda e, hx=hx: e.tensor_tensor(
                        hx[:, :], hx[:, :], brr, op=ALU.add), [tv, t_repl[2]], inc="V")
                    tg = O("scalar", lambda e, hx=hx, pp=pp: e.activation(
                        hbf[pp][:, :], hx[:, :], AF.Gelu, bias=0.0, scale=1.0),
                        [tv, t_hbf_free[pp]], inc="A")
                    t_hsb_free[pp] = tg
                    tv = O("vector", lambda e, pp=pp, tc=tc, e_=e_: e.tensor_scalar(
                        hbf[pp][:, :], hbf[pp][:, :], c_all[:, 8 * tc + e_:8 * tc + e_ + 1], None,
                        op0=ALU.mult), [tg, t_call], inc="V")
                    return tv

                def emit_tail(i, tc, tv_c):
                    nonlocal t_w2_last
                    pp = i % 2
                    tev = None
                    for g in range(2):
                        tmm = None
                        for j in range(8):
                            tmm = O("tensor", lambda e, g=g, j=j, pp=pp: e.transpose(
                                PT[:, 128 * j:128 * (j + 1)],
                                hbf[pp][:, 1024 * g + 128 * j:1024 * g + 128 * (j + 1)], idb[:, :]),
                                [tv_c, pt_free[0]] if j == 0 else [],
                                inc="T" if j == 7 else None)
                        tev = O("vector", lambda e, g=g, pp=pp: e.tensor_copy(
                            hfm[pp][:, 1024 * g:1024 * (g + 1)], PT[:, :]),
                            [tmm, t_hfm_free[pp]], inc="V")
                        pt_free[0] = tev
                    t_hbf_free[pp] = tmm
                    tmm = None
                    for k in range(16):
                        for g in range(2):
                            ws = ([tev, t_w2_dma[k], pc_free[g]] if k == 0 else [])
                            tmm = O("tensor", lambda e, k=k, g=g, pp=pp: e.matmul(
                                PC[:, 512 * g:512 * (g + 1)],
                                hfm[pp][:, 128 * k:128 * (k + 1)],
                                w2e[k][:, 512 * g:512 * (g + 1)],
                                start=(k == 0), stop=(k == 15)),
                                ws,
                                inc="T" if (k == 15 and g == 1) else None)
                    t_hfm_free[pp] = tmm
                    t_w2_last = tmm
                    return tmm, pp

                def emit_yevac(i, tmm):
                    tv = O("vector", lambda e, i=i: e.tensor_tensor(
                        yacc[i][:, :], yacc[i][:, :], PC[:, :], op=ALU.add),
                        [tmm, t_yacc_free[i]], inc="V")
                    pc_free[0] = pc_free[1] = tv
                    t_yacc_free[i] = tv

                t_w2_last = None
                pend = None
                for i, tc in enumerate(tcs):
                    tmm1, tvh, pa_f, pb_f = emit_w1_and_chain(i, tc)
                    t_w1_last = tmm1
                    if pend is not None:
                        j, tvc_j = pend
                        tmm2, _ = emit_tail(j, tcs[j], tvc_j)
                    tv_c = emit_chain(i, tc, e_, tvh)
                    pa_free[0] = pa_free[1] = pa_f
                    pb_free[0] = pb_free[1] = pb_f
                    if pend is not None:
                        j, _ = pend
                        emit_yevac(j, tmm2)
                    pend = (i, tv_c)
                j, tvc_j = pend
                tmm2, _ = emit_tail(j, tcs[j], tvc_j)
                emit_yevac(j, tmm2)
                t_w1_free = [t_w1_last] * NKC
                t_w2_free = [t_w2_last] * 16

            if dbg == 2:
                continue
            tl = DR(lambda e: e.dma_start(out=rowb[:, 0:1024], in_=lng_h[:, :]),
                    [t_rowb_free])
            tg1 = repl_rows(lngr, rowb[:, 0:1024], 1024,
                            wait=[tl, t_hfm_free[0], t_hfm_free[1]])
            tl = DR(lambda e: e.dma_start(out=rowb[:, 1024:2048], in_=lnb_h[:, :]))
            tg2 = repl_rows(lnbr, rowb[:, 1024:2048], 1024, wait=[tl])
            t_rowb_free = tg2
            for i, tc in enumerate(tcs):
                pp = i % 2
                tl = DL(lambda e, tc=tc: e.dma_start(
                    out=xst[:, :], in_=x2_h[128 * tc:128 * (tc + 1), :]),
                    [t_xst_free, t_hsb_free[0], t_hsb_free[1]])
                ts_ = ln_stats(yacc[i], D, sm_c[:, 6:7], sm_c[:, 7:8], wait=[t_yacc_free[i]])
                tv = O("vector", lambda e, i=i, pp=pp: e.tensor_scalar(
                    outsc[pp][:, :], yacc[i][:, :], sm_c[:, 6:7], sm_c[:, 7:8],
                    op0=ALU.subtract, op1=ALU.mult),
                    [ts_, t_hbf_free[pp]], inc="V")
                tv = O("vector", lambda e, pp=pp: e.tensor_tensor(
                    outsc[pp][:, :], outsc[pp][:, :], lngr, op=ALU.mult), [tv, tg1], inc="V")
                tv = O("vector", lambda e, pp=pp: e.tensor_tensor(
                    outsc[pp][:, :], outsc[pp][:, :], lnbr, op=ALU.add), [tv, tg2], inc="V")
                tv = O("vector", lambda e, pp=pp: e.tensor_tensor(
                    outsc[pp][:, :], outsc[pp][:, :], xst[:, :], op=ALU.add), [tv, tl], inc="V")
                t_xst_free = tv
                tst = DX(lambda e, tc=tc, pp=pp: e.dma_start(
                    out=out_h[128 * tc:128 * (tc + 1), :], in_=outsc[pp][:, :]), [tv])
                t_out_st.append(tst)
                t_hbf_free[pp] = tst

        DX(lambda e: e.dma_start(out=x2o_h[:, :], in_=x2_h[:, :]), [t_x2_done])
        if dbg == 2:
            t_out_st = [t_yacc_free[7]]
            for j in range(8):
                DX(lambda e, j=j: e.dma_start(out=dbg_h["d_yacc"][:, 1024 * j:1024 * (j + 1)],
                                              in_=yacc[j][:, :]), [t_out_st[-1]])
            DX(lambda e: e.dma_start(out=dbg_h["d_lngr"][:, :], in_=lngr[:, :]), [t_out_st[-1]])
            DX(lambda e: e.dma_start(out=dbg_h["d_lnbr"][:, :], in_=lnbr[:, :]), [t_out_st[-1]])
            DX(lambda e: e.dma_start(out=dbg_h["d_b1r"][:, :], in_=b1r[:, :]), [t_out_st[-1]])
            DX(lambda e: e.dma_start(out=dbg_h["d_hsb0"][:, :], in_=hsb[0][:, :]), [t_out_st[-1]])
            DX(lambda e: e.dma_start(out=dbg_h["d_hbf0"][:, :], in_=hbf[0][:, :]), [t_out_st[-1]])
            DX(lambda e: e.dma_start(out=dbg_h["d_hfm0"][:, :], in_=hfm[0][:, :]), [t_out_st[-1]])
            for k in range(16):
                DX(lambda e, k=k: e.dma_start(out=dbg_h["d_w2e"][:, 1024 * k:1024 * (k + 1)],
                                              in_=w2e[k][:, :]), [t_out_st[-1]])
            for k in range(8):
                DX(lambda e, k=k: e.dma_start(out=dbg_h["d_w1e"][:, 2048 * k:2048 * (k + 1)],
                                              in_=w1e[k][:, :]), [t_out_st[-1]])
            DX(lambda e: e.dma_start(out=dbg_h["d_gr"][:, :], in_=gr_[:, :]), [t_out_st[-1]])
            DX(lambda e: e.dma_start(out=dbg_h["d_brr"][:, :], in_=brr[:, :]), [t_out_st[-1]])
            DX(lambda e: e.dma_start(out=dbg_h["d_call"][:, :], in_=c_all[:, :]), [t_out_st[-1]])
            DX(lambda e: e.dma_start(out=dbg_h["d_mu2"][:, :], in_=mu2[:, :]), [t_out_st[-1]])
            DX(lambda e: e.dma_start(out=dbg_h["d_rs2"][:, :], in_=rs2[:, :]), [t_out_st[-1]])
            for k in range(NKC):
                DX(lambda e, k=k: e.dma_start(out=dbg_h["d_ln2"][128 * k:128 * (k + 1), :],
                                              in_=lnar[k][:, :]), [t_out_st[-1]])
        kb.wait("sync", [("dX", kb.cnt["dX"]), ("dL", kb.cnt["dL"]),
                         ("dR", kb.cnt["dR"]), ("dT", kb.cnt["dT"])])
        kb.wait("gpsimd", [("dW", kb.cnt["dW"]), ("dW1", kb.cnt["dW1"]),
                           ("dW2", kb.cnt["dW2"])])
        kb.emit_all(block)
    return nc


_CACHE = {}


def kernel(**inputs):
    if "nc" not in _CACHE:
        _CACHE["nc"] = build_nc()
    nc = _CACHE["nc"]
    f32 = np.float32
    x = np.ascontiguousarray(inputs["L0_attn_x"], dtype=f32)
    base = dict(
        wqkv=np.ascontiguousarray(inputs["L0_attn_Wqkv"][0:2], dtype=f32),
        bqkv=np.ascontiguousarray(inputs["L0_attn_bqkv"][0:2], dtype=f32),
        wout=np.ascontiguousarray(inputs["L0_attn_Wout"][0:2], dtype=f32),
        bout=np.ascontiguousarray(inputs["L0_attn_bout"][0:2], dtype=f32),
        gate_w=np.ascontiguousarray(inputs["L0_ffn_gate_w"], dtype=f32),
        w1=np.ascontiguousarray(inputs["L0_ffn_w1"], dtype=f32),
        b1=np.ascontiguousarray(inputs["L0_ffn_b1"], dtype=f32),
        ln1g=np.ascontiguousarray(inputs["L0_ffn_ln1_g"], dtype=f32),
        ln1b=np.ascontiguousarray(inputs["L0_ffn_ln1_b"], dtype=f32),
        w2=np.ascontiguousarray(inputs["L0_ffn_w2"], dtype=f32),
        b2=np.ascontiguousarray(inputs["L0_ffn_b2"], dtype=f32),
        lng=np.ascontiguousarray(inputs["L0_ffn_ln_g"], dtype=f32).reshape(1, D),
        lnb=np.ascontiguousarray(inputs["L0_ffn_ln_b"], dtype=f32).reshape(1, D),
        ident=np.eye(P, dtype=f32),
    )
    in_maps = [dict(base, x=np.ascontiguousarray(x[b])) for b in range(8)]
    res = run_bass_kernel_spmd(nc, in_maps, core_ids=list(range(8)))
    out = np.stack([np.asarray(r["out"]) for r in res.results], axis=0)
    return out.astype(np.float32)


# version 04
# speedup vs baseline: 1.3965x; 1.0596x over previous
import numpy as np
import ml_dtypes
import concourse.bass as bass
import concourse.mybir as mybir
from concourse.bass_utils import run_bass_kernel_spmd

F32 = mybir.dt.float32
F32R = mybir.dt.float32r
BF16 = mybir.dt.bfloat16
I32 = mybir.dt.int32
AF = mybir.ActivationFunctionType
ALU = mybir.AluOpType
AX = mybir.AxisListType

P = 128
T = 2048
D = 1024
NH = 16
SEG = 512
NTC = 16
NKC = 8
NE = 8
HF = 2048
EPS = 1e-5
MAGIC = 0x5F3759DF
ENGINES = ("sync", "gpsimd", "tensor", "vector", "scalar")


class KB:
    def __init__(self, nc):
        self.nc = nc
        self.q = {e: [] for e in ENGINES}
        self.cnt = {}
        self.sems = {}

    def new_sem(self, stack, name):
        self.sems[name] = stack.enter_context(self.nc.semaphore(name))
        self.cnt[name] = 0

    def op(self, eng, fn, waits=(), inc=None, amt=1):
        ws = tuple((s, v) for w in waits if w is not None
                   for (s, v) in [w] if v > 0)

        def emit(engine, ws=ws, fn=fn, inc=inc, amt=amt):
            for s, v in ws:
                engine.wait_ge(self.sems[s], v)
            inst = fn(engine)
            if inc is not None:
                inst.then_inc(self.sems[inc], amt)
        self.q[eng].append(emit)
        if inc is not None:
            self.cnt[inc] += amt
            return (inc, self.cnt[inc])
        return None

    def wait(self, eng, waits):
        ws = tuple((s, v) for w in waits if w is not None
                   for (s, v) in [w] if v > 0)
        if ws:
            def emit(engine, ws=ws):
                for s, v in ws:
                    engine.wait_ge(self.sems[s], v)
            self.q[eng].append(emit)

    def emit_all(self, block):
        for name in ENGINES:
            def body(engine, name=name):
                for emit in self.q[name]:
                    emit(engine)
            getattr(block, name)(body)


def build_nc(dbg=False, ne_dbg=NE):
    nc = bass.Bass("TRN2", target_bir_lowering=False)
    x_h = nc.dram_tensor("x", [T, D], F32, kind="ExternalInput").ap()
    wqkv_h = nc.dram_tensor("wqkv", [2, 3 * D, D], F32, kind="ExternalInput").ap()
    bqkv_h = nc.dram_tensor("bqkv", [2, 3 * D], F32, kind="ExternalInput").ap()
    wout_h = nc.dram_tensor("wout", [2, D, D], F32, kind="ExternalInput").ap()
    bout_h = nc.dram_tensor("bout", [2, D], F32, kind="ExternalInput").ap()
    gw_h = nc.dram_tensor("gate_w", [D, NE], F32, kind="ExternalInput").ap()
    w1_h = nc.dram_tensor("w1", [NE, D, HF], F32, kind="ExternalInput").ap()
    b1_h = nc.dram_tensor("b1", [NE, HF], F32, kind="ExternalInput").ap()
    l1g_h = nc.dram_tensor("ln1g", [NE, HF], F32, kind="ExternalInput").ap()
    l1b_h = nc.dram_tensor("ln1b", [NE, HF], F32, kind="ExternalInput").ap()
    w2_h = nc.dram_tensor("w2", [NE, HF, D], F32, kind="ExternalInput").ap()
    b2_h = nc.dram_tensor("b2", [NE, D], F32, kind="ExternalInput").ap()
    lng_h = nc.dram_tensor("lng", [1, D], F32, kind="ExternalInput").ap()
    lnb_h = nc.dram_tensor("lnb", [1, D], F32, kind="ExternalInput").ap()
    id_h = nc.dram_tensor("ident", [P, P], F32, kind="ExternalInput").ap()
    out_h = nc.dram_tensor("out", [T, D], F32, kind="ExternalOutput").ap()
    x2o_h = nc.dram_tensor("x2dbg", [T, D], F32, kind="ExternalOutput").ap()
    dbg_h = {}
    if dbg == 2:
        for nm, shape, dt in [("d_call", [P, P], F32), ("d_ln2", [D, T], BF16),
                              ("d_mu2", [P, NTC], F32), ("d_rs2", [P, NTC], F32),
                              ("d_yacc", [P, 8192], F32), ("d_lngr", [P, D], F32),
                              ("d_lnbr", [P, D], F32), ("d_b1r", [P, HF], F32),
                              ("d_gr", [P, HF], F32), ("d_brr", [P, HF], F32),
                              ("d_hsb0", [P, HF], F32), ("d_hbf0", [P, HF], BF16),
                              ("d_hfm0", [P, HF], BF16), ("d_w2e", [P, 16 * 1024], BF16),
                              ("d_w1e", [P, 16 * 1024], BF16)]:
            dbg_h[nm] = nc.dram_tensor(nm, shape, dt, kind="ExternalOutput").ap()
    if dbg == 1:
        for nm, shape, dt in [("d_lnar", [D, T], BF16), ("d_qf", [P, 4096], BF16),
                              ("d_kf", [P, 4096], BF16), ("d_vt", [P, 4096], BF16),
                              ("d_otm", [P, 4096], BF16), ("d_ofm0", [D, D], BF16),
                              ("d_ofm1", [D, D], BF16), ("d_zt", [P, 16], F32),
                              ("d_wo0", [D, D], BF16), ("d_wo1", [D, D], BF16)]:
            dbg_h[nm] = nc.dram_tensor(nm, shape, dt, kind="ExternalOutput").ap()
    wqkvb_h = nc.dram_tensor("wqkvb", [2, 3 * D, D], BF16, kind="Internal").ap()
    woutb_h = nc.dram_tensor("woutb", [2, D, D], BF16, kind="Internal").ap()
    x2_h = nc.dram_tensor("x2scr", [T, D], F32, kind="Internal").ap()

    xp_h = x_h.rearrange("(c j par) d -> c par j d", j=64, par=2)
    x2p_h = x2_h.rearrange("(c j par) d -> c par j d", j=64, par=2)

    from contextlib import ExitStack
    with ExitStack() as stack:
        def sbt(name, shape, dt):
            return stack.enter_context(nc.sbuf_tensor(name, shape, dt))

        def pstt(name, shape, dt):
            return stack.enter_context(nc.psum_tensor(name, shape, dt))

        idb = sbt("idb", [P, P], BF16)
        idf = sbt("idf", [P, P], F32)
        ones_r = sbt("ones_r", [1, P], F32)
        ones_rb = sbt("ones_rb", [1, P], BF16)
        onecb = sbt("onecb", [P, 1], BF16)
        rowbb = sbt("rowbb", [1, HF], BF16)
        onec = sbt("onec", [P, 1], F32)
        bstg = sbt("bstg", [24, P], F32)
        bq = sbt("bq", [P, 48], F32)
        c_all = sbt("c_all", [P, P], F32)
        mu2 = sbt("mu2", [P, NTC], F32)
        rs2 = sbt("rs2", [P, NTC], F32)
        gwt = sbt("gwt", [P, 64], F32)
        csr = sbt("csr", [P, NE], F32)
        cs_row = sbt("cs_row", [1, NE], F32)
        U8 = sbt("U8", [8, 2048], F32)
        sm_a = sbt("sm_a", [P, 16], F32)
        sm_b = sbt("sm_b", [P, 16], F32)
        sm_c = sbt("sm_c", [P, 16], F32)
        sm_i = sbt("sm_i", [P, 16], I32)
        bn_raw = sbt("bn_raw", [P, 24], F32)
        bn_ag = sbt("bn_ag", [P, 2], F32)
        g8_a = sbt("g8_a", [P, NE], F32)
        g8_b = sbt("g8_b", [P, NE], F32)
        g8_d = sbt("g8_d", [P, NE], F32)
        zt = sbt("zt", [P, 16], F32)
        lnar = [sbt(f"lnar{k}", [P, T], BF16) for k in range(NKC)]
        arw = [sbt(f"arw{k}", [P, 4096], BF16) for k in range(NKC)]
        arx = [sbt(f"arx{k}", [P, 4096], BF16) for k in range(12)]

        wqkT = [arw[k][:, 0:3072] for k in range(NKC)]
        woT = [[arw[k][:, 3072:4096] for k in range(NKC)],
               [arx[10 + j // 4][:, (j % 4) * 1024:(j % 4 + 1) * 1024] for j in range(NKC)]]
        ofm = [[arx[0 + j // 4][:, (j % 4) * 1024:(j % 4 + 1) * 1024] for j in range(NKC)],
               [arx[2 + j // 4][:, (j % 4) * 1024:(j % 4 + 1) * 1024] for j in range(NKC)]]
        qf = arx[4]
        lntm = arx[4][:, 2048:3072]
        kf = arx[5]
        vt = arx[6]
        otm = arx[7]
        et = [arx[8][:, 0:2048], arx[8][:, 2048:4096]]
        bvr = arx[9][:, 0:2048].bitcast(F32)
        bo3 = bvr
        xst = arx[9][:, 2048:4096].bitcast(F32)
        wtmp = arx[7][:, 0:2048].bitcast(F32)
        yacc = [arx[j // 2][:, (j % 2) * 2048:(j % 2 + 1) * 2048].bitcast(F32) for j in range(8)]
        w1e = [arw[k][:, 0:2048] for k in range(NKC)]
        w2e = [arw[j // 2][:, 2048 + (j % 2) * 1024:2048 + (j % 2 + 1) * 1024] for j in range(16)]
        b1r = arx[4][:, :].bitcast(F32)
        gr_ = arx[5][:, :].bitcast(F32)
        brr = arx[6][:, :].bitcast(F32)
        x2t = arx[7][:, 0:2048].bitcast(F32)
        hsb = [arx[8][:, :].bitcast(F32), arx[9][:, :].bitcast(F32)]
        hbf = [arx[11][:, 0:2048], arx[11][:, 2048:4096]]
        hfm = [arx[10][:, 0:2048], arx[10][:, 2048:4096]]
        rowb = U8[0:1, 0:2048]
        cta = U8[:, 0:1024]
        b2t = U8[0:8, 1024:2048]
        lngr = arx[10][:, 0:2048].bitcast(F32)
        lnbr = arx[10][:, 2048:4096].bitcast(F32)
        outsc = [arx[11][:, 0:2048].bitcast(F32), arx[11][:, 2048:4096].bitcast(F32)]

        PA = pstt("PA", [P, 1024], F32)
        PB = pstt("PB", [P, 1024], F32)
        PC = pstt("PC", [P, 1024], F32)
        PD = pstt("PD", [P, 512], F32)
        PT = pstt("PT", [P, 1024], BF16)

        kb = KB(nc)
        for s in ("dX", "dW", "dW1", "dW2", "dL", "dR", "dT", "T", "V", "A"):
            kb.new_sem(stack, s)
        block = stack.enter_context(nc.Block())
        O = kb.op

        def DX(fn, waits=()):
            return O("sync", fn, waits, inc="dX", amt=16)

        def DW(fn, waits=()):
            return O("gpsimd", fn, waits, inc="dW", amt=16)

        def DL(fn, waits=()):
            return O("sync", fn, waits, inc="dL", amt=16)

        last_dr = [None]

        def DR(fn, waits=()):
            tk = O("sync", fn, list(waits) + [last_dr[0]], inc="dR", amt=16)
            last_dr[0] = tk
            return tk

        def DT(fn, waits=()):
            return O("sync", fn, waits, inc="dT", amt=16)

        pd_free = [None]

        def newton_rsqrt(var_ap, out_ap, n, wait=()):
            vpe = sm_a[:, 0:n]
            t2 = sm_b[:, 0:n]
            ii = sm_i[:, 0:n]
            tk = O("vector", lambda e: e.tensor_scalar(vpe, var_ap, EPS, None, op0=ALU.add), wait, inc="V")
            tk = O("vector", lambda e: e.tensor_scalar(ii, vpe.bitcast(I32), 1, None, op0=ALU.arith_shift_right), [tk], inc="V")
            tk = O("vector", lambda e: e.tensor_scalar(ii, ii, -1.0, float(MAGIC), op0=ALU.mult, op1=ALU.add), [tk], inc="V")
            tk = O("vector", lambda e: e.tensor_copy(out_ap, ii.bitcast(F32)), [tk], inc="V")
            for _ in range(2):
                tk = O("vector", lambda e: e.tensor_tensor(t2, out_ap, out_ap, op=ALU.mult), [tk], inc="V")
                tk = O("vector", lambda e: e.tensor_tensor(t2, t2, vpe, op=ALU.mult), [tk], inc="V")
                tk = O("vector", lambda e: e.tensor_scalar(t2, t2, -0.5, 1.5, op0=ALU.mult, op1=ALU.add), [tk], inc="V")
                tk = O("vector", lambda e: e.tensor_tensor(out_ap, out_ap, t2, op=ALU.mult), [tk], inc="V")
            return tk

        def ln_stats(src_ap, width, mu_ap, rstd_ap, wait=()):
            ng = width // 512
            tk = None
            for i in range(ng):
                tk = O("vector", lambda e, i=i: e.bn_stats(
                    bn_raw[:, 6 * i:6 * i + 6], src_ap[:, 512 * i:512 * (i + 1)]),
                    wait if i == 0 else [], inc="V")
            tk = O("vector", lambda e, ng=ng: e.bn_aggr(bn_ag[:, :], bn_raw[:, 0:6 * ng]), [tk], inc="V")
            tk = O("vector", lambda e: e.tensor_copy(mu_ap, bn_ag[:, 0:1]), [tk], inc="V")
            return newton_rsqrt(bn_ag[:, 1:2], rstd_ap, 1, wait=[tk])

        def repl_rows(dst_f32, row_ap, width, wait=()):
            rb = rowbb[:, 0:width]
            tc_ = O("vector", lambda e, rb=rb, row_ap=row_ap: e.tensor_copy(rb, row_ap),
                    list(wait), inc="V")
            tk_out = None
            for j in range(0, width, 512):
                w = min(512, width - j)
                tm = O("tensor", lambda e, j=j, w=w, rb=rb: e.matmul(
                    PD[:, 0:w], ones_rb[:, :], rb[:, j:j + w],
                    start=True, stop=True),
                    [tc_, pd_free[0]], inc="T")
                tk_out = O("vector", lambda e, j=j, w=w: e.tensor_copy(
                    dst_f32[:, j:j + w], PD[:, 0:w]), [tm], inc="V")
                pd_free[0] = tk_out
            return tk_out

        t_ones = O("vector", lambda e: e.memset(ones_r[:, :], 1.0), inc="V")
        O("vector", lambda e: e.memset(ones_rb[:, :], 1.0), inc="V")
        O("vector", lambda e: e.memset(onecb[:, :], 1.0), inc="V")
        t_onec = O("vector", lambda e: e.memset(onec[:, :], 1.0), inc="V")
        t_id = DR(lambda e: e.dma_start(out=idf[:, :], in_=id_h[:, :]))
        t_idb = DW(lambda e: e.dma_start(out=idb[:, :], in_=id_h[:, :]))
        t_wqc = [DW(lambda e, b=b: e.dma_start(out=wqkvb_h[b], in_=wqkv_h[b])) for b in range(2)]
        t_woc = [DW(lambda e, b=b: e.dma_start(out=woutb_h[b], in_=wout_h[b])) for b in range(2)]
        t_bq = []
        tprev = None
        for br in range(2):
            tl = DR(lambda e, br=br: e.dma_start(
                out=bstg[:, :], in_=bqkv_h[br].rearrange("(c p) -> c p", p=128)),
                [tprev])
            tt_ = O("tensor", lambda e: e.transpose(PD[:, 0:24], bstg[:, :], idf[0:24, 0:24]),
                    [tl, t_id, pd_free[0]], inc="T")
            tk = O("vector", lambda e, br=br: e.tensor_copy(bq[:, 24 * br:24 * br + 24], PD[:, 0:24]),
                   [tt_], inc="V")
            pd_free[0] = tk
            tprev = tk
            t_bq.append(tk)
        t_gw = DR(lambda e: e.dma_start(
            out=gwt[:, :].rearrange("p (c e) -> p c e", e=NE),
            in_=gw_h.rearrange("(c p) e -> p c e", p=128)))
        tm = None
        for kc in range(NKC):
            tm = O("tensor", lambda e, kc=kc: e.matmul(
                PD[0:1, 0:NE], onec[:, :], gwt[:, 8 * kc:8 * kc + 8],
                start=(kc == 0), stop=(kc == NKC - 1)),
                [t_gw, t_onec, pd_free[0]], inc="T" if kc == NKC - 1 else None)
        t_csrow = O("vector", lambda e: e.tensor_copy(cs_row[:, :], PD[0:1, 0:NE]), [tm], inc="V")
        pd_free[0] = t_csrow
        tm = O("tensor", lambda e: e.matmul(PD[:, 0:NE], ones_r[:, :], cs_row[:, :],
                                            start=True, stop=True), [t_csrow, t_ones, pd_free[0]], inc="T")
        t_csr = O("vector", lambda e: e.tensor_copy(csr[:, :], PD[:, 0:NE]), [tm], inc="V")
        pd_free[0] = t_csr

        t_lnar = None
        t_arw_free = None
        t_ofm_free = None
        t_xst_free = None
        t_wtmp_free = None
        t_x2_done = None
        pa_free = [None, None]
        pb_free = [None, None]
        pc_free = [None, None]
        pt_free = [None]
        t_et_free = [None, None]
        t_otr_done = None
        t_wq_dma = [None] * NKC
        t_bvr = None
        t_woT_ready = None
        t_wo_free = [None, None]

        _seg_state = {}
        for span in range(2):
            base = span * 1024
            for i in range(8):
                tc = span * 8 + i
                tl = DL(lambda e, tc=tc: e.dma_start(
                    out=xst[:, :], in_=x_h[128 * tc:128 * (tc + 1), :]), [t_xst_free])
                ts_ = ln_stats(xst, D, sm_c[:, 0:1], sm_c[:, 1:2], wait=[tl])
                ta = O("vector", lambda e: e.tensor_scalar(
                    lntm[:, :], xst[:, :], sm_c[:, 0:1], sm_c[:, 1:2],
                    op0=ALU.subtract, op1=ALU.mult), [ts_], inc="V")
                t_xst_free = ta
                tmm = None
                for kc in range(NKC):
                    tmm = O("tensor", lambda e, kc=kc: e.transpose(
                        PT[:, 128 * kc:128 * (kc + 1)], lntm[:, 128 * kc:128 * (kc + 1)], idb[:, :]),
                        [ta, t_idb, pt_free[0]] if kc == 0 else [],
                        inc="T" if kc == NKC - 1 else None)
                tev = None
                for kc in range(NKC):
                    tev = O("vector", lambda e, kc=kc, tc=tc: e.tensor_copy(
                        lnar[kc][:, 128 * tc:128 * (tc + 1)], PT[:, 128 * kc:128 * (kc + 1)]),
                        [tmm], inc="V")
                pt_free[0] = tev
                t_lnar = tev

            for br in range(2):
                for kc in range(NKC):
                    t_wq_dma[kc] = DT(lambda e, kc=kc, br=br: e.dma_start(
                        out=wqkT[kc][:, :], in_=wqkvb_h[br][:, 128 * kc:128 * (kc + 1)],
                        transpose=True), [t_wqc[br], t_arw_free])
                if span == 0:
                    for kc in range(NKC):
                        t_woT_ready = DT(lambda e, kc=kc, br=br: e.dma_start(
                            out=woT[br][kc][:, :], in_=woutb_h[br][:, 128 * kc:128 * (kc + 1)],
                            transpose=True), [t_woc[br]])
                tl = DR(lambda e, br=br: e.dma_start(
                    out=rowb[:, 0:1024], in_=bqkv_h[br:br + 1, 2048:3072]), [t_bvr])
                t_bvr = repl_rows(bvr, rowb[:, 0:1024], 1024, wait=[tl])

                for seg in range(2):
                    if br == 0:
                        co, st_ = base + SEG * seg, 1
                    else:
                        co, st_ = base + seg, 2

                    t_qk_ready = None
                    for fc in range(16):
                        half = fc % 2
                        pa = PA[:, 512 * half:512 * (half + 1)]
                        tmm = None
                        for kc in range(NKC):
                            rhs_ap = lnar[kc][:, co:co + st_ * (SEG - 1) + 1:st_]
                            ws = ([t_wq_dma[NKC - 1], t_lnar, pa_free[half]] if kc == 0 else [])
                            tmm = O("tensor", lambda e, kc=kc, fc=fc, pa=pa, rhs_ap=rhs_ap: e.matmul(
                                pa, wqkT[kc][:, 128 * fc:128 * (fc + 1)], rhs_ap,
                                start=(kc == 0), stop=(kc == NKC - 1)),
                                ws,
                                inc="T" if kc == NKC - 1 else None)
                        dst = qf if fc < 8 else kf
                        dco = 512 * (fc % 8)
                        t_qk_ready = O("vector", lambda e, pa=pa, dst=dst, dco=dco, fc=fc, br=br: e.tensor_scalar(
                            dst[:, dco:dco + 512], pa, bq[:, 24 * br + fc:24 * br + fc + 1], None,
                            op0=ALU.add), [tmm, t_bq[br]], inc="V")
                        pa_free[half] = t_qk_ready
                    t_v_ready = None
                    for j in range(4):
                        tmm = None
                        for kc in range(NKC):
                            lhs_ap = lnar[kc][:, co + st_ * 128 * j:co + st_ * (128 * (j + 1) - 1) + 1:st_]
                            for g in range(2):
                                ws = ([t_wq_dma[NKC - 1], t_lnar, pb_free[g]] if kc == 0 else [])
                                tmm = O("tensor", lambda e, kc=kc, g=g, lhs_ap=lhs_ap: e.matmul(
                                    PB[:, 512 * g:512 * (g + 1)], lhs_ap,
                                    wqkT[kc][:, 2048 + 512 * g:2048 + 512 * (g + 1)],
                                    start=(kc == 0), stop=(kc == NKC - 1)),
                                    ws,
                                    inc="T" if (kc == NKC - 1 and g == 1) else None)
                        t_v_ready = O("vector", lambda e, j=j: e.tensor_tensor(
                            vt[:, 1024 * j:1024 * (j + 1)], PB[:, :], bvr, op=ALU.add),
                            [tmm, t_bvr], inc="V")
                        pb_free[0] = pb_free[1] = t_v_ready
                    t_arw_free = tmm

                    t_crec = [None, None]
                    if "pvf" not in _seg_state:
                        _seg_state["pvf"] = [None, None, None, None]
                    pvf = _seg_state["pvf"]

                    def emit_scores(h, t_qk_ready=t_qk_ready):
                        hp = 64 * (h % 2)
                        qh = qf[hp:hp + 64, 512 * (h // 2):512 * (h // 2) + 512]
                        kh = kf[hp:hp + 64, 512 * (h // 2):512 * (h // 2) + 512]
                        eb = et[h % 2]
                        tE = None
                        for kc in range(4):
                            half = kc % 2
                            pb = PB[:, 512 * half:512 * (half + 1)]
                            tm1 = O("tensor", lambda e, qh=qh, kh=kh, kc=kc, pb=pb: e.matmul(
                                pb, kh[:, 128 * kc:128 * (kc + 1)], qh, start=True, stop=True),
                                [t_qk_ready, pb_free[half], t_et_free[h % 2]], inc="T")
                            tE = O("scalar", lambda e, pb=pb, eb=eb, kc=kc: e.activation(
                                eb[:, 512 * kc:512 * (kc + 1)], pb, AF.Exp, bias=0.0, scale=0.125),
                                [tm1], inc="A")
                            pb_free[half] = tE
                        return (h, tE)

                    def emit_pv(state, t_v_ready=t_v_ready):
                        h, tE = state
                        pi = 8 * (h % 2)
                        eb = et[h % 2]
                        last_ev = None
                        tpv = [None] * 4
                        tzz = None
                        for qc in range(4):
                            pcs = PC[:, 256 * qc:256 * qc + 64]
                            for kc in range(4):
                                ws = ([tE, t_v_ready, pvf[qc], t_crec[h % 2]]
                                      if kc == 0 else [])
                                tpv[qc] = O("tensor", lambda e, qc=qc, kc=kc, h=h, eb=eb, pcs=pcs: e.matmul(
                                    pcs, eb[:, 512 * kc + 128 * qc:512 * kc + 128 * (qc + 1)],
                                    vt[:, 1024 * kc + 64 * h:1024 * kc + 64 * (h + 1)],
                                    start=(kc == 0), stop=(kc == 3)),
                                    ws, inc="T" if kc == 3 else None)
                                tzz = O("tensor", lambda e, qc=qc, kc=kc, eb=eb, pi=pi: e.matmul(
                                    PD[:, pi // 2 + qc:pi // 2 + qc + 1],
                                    eb[:, 512 * kc + 128 * qc:512 * kc + 128 * (qc + 1)],
                                    onecb[:, :],
                                    start=(kc == 0), stop=(kc == 3)),
                                    [], inc="T" if kc == 3 else None)
                        tc_ = []
                        for qc in range(4):
                            t1 = O("vector", lambda e, qc=qc, pi=pi: e.tensor_scalar(
                                zt[:, pi + qc:pi + qc + 1], PD[:, pi // 2 + qc:pi // 2 + qc + 1],
                                3.0, None, op0=ALU.mult), [tzz] if qc == 0 else [], inc="V")
                            t2 = O("vector", lambda e, qc=qc, pi=pi: e.reciprocal(
                                zt[:, pi + 4 + qc:pi + 5 + qc], zt[:, pi + qc:pi + qc + 1]),
                                [t1], inc="V")
                            tc_.append(t2)
                        t_crec[h % 2] = tc_[3]
                        for qc in range(4):
                            pcs = PC[:, 256 * qc:256 * qc + 64]
                            last_ev = O("vector", lambda e, qc=qc, h=h, pcs=pcs, pi=pi: e.tensor_scalar(
                                otm[:, 1024 * qc + 64 * h:1024 * qc + 64 * (h + 1)], pcs,
                                zt[:, pi + 4 + qc:pi + 5 + qc], None, op0=ALU.mult),
                                [tpv[qc], tc_[qc], t_otr_done], inc="V")
                            pvf[qc] = last_ev
                        t_et_free[h % 2] = tzz
                        return last_ev

                    prev = None
                    t_seg_oevac = None
                    for h in range(NH):
                        st8 = emit_scores(h)
                        if prev is not None:
                            t_seg_oevac = emit_pv(prev)
                        prev = st8
                    t_seg_oevac = emit_pv(prev)
                    pd_free[0] = t_crec[1]
                    pc_free[0] = pc_free[1] = t_seg_oevac

                    for fc in range(NKC):
                        tmm = None
                        for qc in range(4):
                            tmm = O("tensor", lambda e, fc=fc, qc=qc: e.transpose(
                                PT[:, 128 * qc:128 * (qc + 1)],
                                otm[:, 1024 * qc + 128 * fc:1024 * qc + 128 * (fc + 1)], idb[:, :]),
                                [t_seg_oevac, pt_free[0]] if qc == 0 else [],
                                inc="T" if qc == 3 else None)
                        if br == 0:
                            dst = ofm[0][fc][:, 512 * seg:512 * (seg + 1)].rearrange(
                                "p (b par j) -> p b j par", b=4, par=2, j=64)
                        else:
                            dst = ofm[1][fc].rearrange(
                                "p (jhi par jlo) -> p jhi par jlo", jhi=8, par=2, jlo=64)[:, :, seg, :]
                        tev = O("vector", lambda e, dst=dst: e.tensor_copy(
                            dst, PT[:, 0:512]), [tmm], inc="V")
                        pt_free[0] = tev
                    t_otr_done = tev

            tl0 = DR(lambda e: e.dma_start(out=rowb[:, 0:1024], in_=bout_h[0:1, :]), [t_bvr])
            tl1 = DR(lambda e: e.dma_start(out=rowb[:, 1024:2048], in_=bout_h[1:2, :]))
            tr0 = O("vector", lambda e: e.tensor_tensor(
                rowb[:, 0:1024], rowb[:, 0:1024], rowb[:, 1024:2048], op=ALU.add),
                [tl0, tl1], inc="V")
            tr0 = O("vector", lambda e: e.tensor_scalar(
                rowb[:, 0:1024], rowb[:, 0:1024], 1.0 / 3.0, None, op0=ALU.mult), [tr0], inc="V")
            t_bo3 = repl_rows(bo3, rowb[:, 0:1024], 1024, wait=[tr0])
            t_bvr = t_bo3

            tm0 = tm1 = None
            for i in range(8):
                tc = span * 8 + i
                tl = DL(lambda e, tc=tc: e.dma_start(
                    out=xst[:, :], in_=xp_h[tc]),
                    [t_xst_free])
                for kc in range(NKC):
                    l0 = ofm[0][kc][:, 128 * i:128 * (i + 1)]
                    for g in range(2):
                        ws = ([t_otr_done, pa_free[g], t_woT_ready] if kc == 0 else [])
                        tm0 = O("tensor", lambda e, kc=kc, g=g, l0=l0: e.matmul(
                            PA[:, 512 * g:512 * (g + 1)], l0, woT[0][kc][:, 512 * g:512 * (g + 1)],
                            start=(kc == 0), stop=(kc == NKC - 1)),
                            ws,
                            inc="T" if (kc == NKC - 1 and g == 1) else None)
                for kc in range(NKC):
                    l1 = ofm[1][kc][:, 128 * i:128 * (i + 1)]
                    for g in range(2):
                        ws = ([t_otr_done, pb_free[g], t_woT_ready] if kc == 0 else [])
                        tm1 = O("tensor", lambda e, kc=kc, g=g, l1=l1: e.matmul(
                            PB[:, 512 * g:512 * (g + 1)], l1, woT[1][kc][:, 512 * g:512 * (g + 1)],
                            start=(kc == 0), stop=(kc == NKC - 1)),
                            ws,
                            inc="T" if (kc == NKC - 1 and g == 1) else None)
                tv = O("vector", lambda e: e.tensor_tensor(wtmp[:, :], PA[:, :], bo3, op=ALU.add),
                       [tm0, t_bo3, t_wtmp_free], inc="V")
                pa_free[0] = pa_free[1] = tv
                tv = O("vector", lambda e: e.tensor_tensor(wtmp[:, :], wtmp[:, :], PB[:, :], op=ALU.add),
                       [tm1, tv], inc="V")
                pb_free[0] = pb_free[1] = tv
                tv = O("vector", lambda e: e.tensor_tensor(wtmp[:, :], wtmp[:, :], xst[:, :], op=ALU.add),
                       [tl, tv], inc="V")
                t_xst_free = tv
                tst = DX(lambda e, tc=tc: e.dma_start(
                    out=x2p_h[tc], in_=wtmp[:, :]), [tv])
                t_wtmp_free = tst
                t_x2_done = tst
            t_ofm_free = tm1
            t_wo_free = [tm0, tm1]

        if dbg == 1:
            for k in range(NKC):
                DX(lambda e, k=k: e.dma_start(out=dbg_h["d_lnar"][128 * k:128 * (k + 1), :],
                                              in_=lnar[k][:, :]), [t_x2_done])
            DX(lambda e: e.dma_start(out=dbg_h["d_qf"][:, :], in_=qf[:, :]), [t_x2_done])
            DX(lambda e: e.dma_start(out=dbg_h["d_kf"][:, :], in_=kf[:, :]), [t_x2_done])
            DX(lambda e: e.dma_start(out=dbg_h["d_vt"][:, :], in_=vt[:, :]), [t_x2_done])
            DX(lambda e: e.dma_start(out=dbg_h["d_otm"][:, :], in_=otm[:, :]), [t_x2_done])
            for k in range(NKC):
                DX(lambda e, k=k: e.dma_start(out=dbg_h["d_ofm0"][128 * k:128 * (k + 1), :],
                                              in_=ofm[0][k][:, :]), [t_x2_done])
                DX(lambda e, k=k: e.dma_start(out=dbg_h["d_ofm1"][128 * k:128 * (k + 1), :],
                                              in_=ofm[1][k][:, :]), [t_x2_done])
            DX(lambda e: e.dma_start(out=dbg_h["d_zt"][:, :], in_=zt[:, :]), [t_x2_done])
            for k in range(NKC):
                DX(lambda e, k=k: e.dma_start(out=dbg_h["d_wo0"][128 * k:128 * (k + 1), :],
                                              in_=woT[0][k][:, :]), [t_x2_done])
                DX(lambda e, k=k: e.dma_start(out=dbg_h["d_wo1"][128 * k:128 * (k + 1), :],
                                              in_=woT[1][k][:, :]), [t_x2_done])
            DX(lambda e: e.dma_start(out=x2o_h[:, :], in_=x2_h[:, :]), [t_x2_done])
            kb.wait("sync", [("dX", kb.cnt["dX"]), ("dL", kb.cnt["dL"]),
                             ("dR", kb.cnt["dR"]), ("dT", kb.cnt["dT"])])
            kb.wait("gpsimd", [("dW", kb.cnt["dW"])])
            kb.emit_all(block)
            return nc
        t_call = None
        t_call_tc = [None] * NTC
        t_lnar2 = None
        t_lnar2_tc = [None] * NTC
        t_x2t_free = t_wtmp_free

        for tc in range(NTC):
            tl = DL(lambda e, tc=tc: e.dma_start(
                out=xst[:, :], in_=x2_h[128 * tc:128 * (tc + 1), :]),
                [t_xst_free, t_x2_done])
            ts_ = ln_stats(xst, D, mu2[:, tc:tc + 1], rs2[:, tc:tc + 1], wait=[tl])
            ta = O("vector", lambda e, tc=tc: e.tensor_scalar(
                lntm[:, :], xst[:, :], mu2[:, tc:tc + 1], rs2[:, tc:tc + 1],
                op0=ALU.subtract, op1=ALU.mult), [ts_], inc="V")
            tmm = None
            for kc in range(NKC):
                tmm = O("tensor", lambda e, kc=kc: e.transpose(
                    PT[:, 128 * kc:128 * (kc + 1)], lntm[:, 128 * kc:128 * (kc + 1)], idb[:, :]),
                    [ta, pt_free[0]] if kc == 0 else [],
                    inc="T" if kc == NKC - 1 else None)
            tev = None
            for kc in range(NKC):
                tev = O("vector", lambda e, kc=kc, tc=tc: e.tensor_copy(
                    lnar[kc][:, 128 * tc:128 * (tc + 1)], PT[:, 128 * kc:128 * (kc + 1)]),
                    [tmm], inc="V")
            pt_free[0] = tev
            t_lnar2 = tev
            t_lnar2_tc[tc] = tev
            t_x2t_ready = None
            for g in range(2):
                tmm = None
                for kc in range(4 * g, 4 * g + 4):
                    tmm = O("tensor", lambda e, kc=kc, g=g: e.transpose(
                        PC[:, 512 * g + 128 * (kc % 4):512 * g + 128 * (kc % 4 + 1)],
                        xst[:, 128 * kc:128 * (kc + 1)], idf[:, :]),
                        [ts_, pc_free[g], t_x2t_free] if kc % 4 == 0 else [],
                        inc="T" if kc % 4 == 3 else None)
                t_x2t_ready = O("vector", lambda e, g=g: e.tensor_copy(
                    x2t[:, 512 * g:512 * (g + 1)], PC[:, 512 * g:512 * (g + 1)]), [tmm], inc="V")
                pc_free[g] = t_x2t_ready
            t_xst_free = tmm
            tmm = None
            for kc in range(NKC):
                tmm = O("tensor", lambda e, kc=kc: e.matmul(
                    PD[:, 0:NE], x2t[:, 128 * kc:128 * (kc + 1)], gwt[:, 8 * kc:8 * kc + 8],
                    start=(kc == 0), stop=(kc == NKC - 1)),
                    [t_x2t_ready, t_gw, pd_free[0]] if kc == 0 else [],
                    inc="T" if kc == NKC - 1 else None)
            t_x2t_free = tmm
            tk = O("vector", lambda e, tc=tc: e.tensor_scalar(
                g8_a[:, :], csr[:, :], mu2[:, tc:tc + 1], None, op0=ALU.mult),
                [t_csr, ts_], inc="V")
            tk = O("vector", lambda e: e.tensor_tensor(g8_b[:, :], PD[:, 0:NE], g8_a[:, :],
                                                       op=ALU.subtract), [tmm, tk], inc="V")
            pd_free[0] = tk
            tk = O("vector", lambda e, tc=tc: e.tensor_scalar(
                g8_b[:, :], g8_b[:, :], rs2[:, tc:tc + 1], None, op0=ALU.mult), [tk], inc="V")
            tk = O("vector", lambda e: e.reduce_max(sm_c[:, 0:1], g8_b[:, :], axis=AX.X), [tk], inc="V")
            tk = O("vector", lambda e: e.tensor_scalar(g8_a[:, :], g8_b[:, :], sm_c[:, 0:1], None,
                                                       op0=ALU.is_equal), [tk], inc="V")
            tk = O("vector", lambda e: e.scalar_tensor_tensor(
                g8_d[:, :], g8_a[:, :], -1e30, g8_b[:, :], op0=ALU.mult, op1=ALU.add), [tk], inc="V")
            tk = O("vector", lambda e: e.reduce_max(sm_c[:, 1:2], g8_d[:, :], axis=AX.X), [tk], inc="V")
            tk = O("vector", lambda e: e.tensor_tensor(sm_c[:, 2:3], sm_c[:, 0:1], sm_c[:, 1:2],
                                                       op=ALU.subtract), [tk], inc="V")
            tk = O("scalar", lambda e: e.activation(sm_c[:, 3:4], sm_c[:, 2:3], AF.Exp,
                                                    bias=0.0, scale=-1.0), [tk], inc="A")
            tk = O("vector", lambda e: e.tensor_scalar(sm_c[:, 3:4], sm_c[:, 3:4], 1.0, None,
                                                       op0=ALU.add), [tk], inc="V")
            tk = O("vector", lambda e: e.reciprocal(sm_c[:, 4:5], sm_c[:, 3:4]), [tk], inc="V")
            tk = O("vector", lambda e: e.tensor_scalar(sm_c[:, 5:6], sm_c[:, 4:5], -1.0, 1.0,
                                                       op0=ALU.mult, op1=ALU.add), [tk], inc="V")
            tk = O("vector", lambda e: e.tensor_scalar(g8_d[:, :], g8_b[:, :], sm_c[:, 1:2], None,
                                                       op0=ALU.is_equal), [tk], inc="V")
            tk = O("vector", lambda e: e.tensor_scalar(g8_a[:, :], g8_a[:, :], sm_c[:, 4:5], None,
                                                       op0=ALU.mult), [tk], inc="V")
            t_call = O("vector", lambda e, tc=tc: e.scalar_tensor_tensor(
                c_all[:, 8 * tc:8 * (tc + 1)], g8_d[:, :], sm_c[:, 5:6], g8_a[:, :],
                op0=ALU.mult, op1=ALU.add), [tk], inc="V")
            t_call_tc[tc] = t_call

        t_w1_free = [t_arw_free] * NKC
        t_w2_free = [t_wo_free[0]] * 16
        t_w1_dma = [None] * NKC
        t_w2_dma = [None] * 16
        t_yacc_free = [t_ofm_free] * 8
        t_rowb_free = None
        t_hsb_free = [t_et_free[1] or t_et_free[0], t_x2_done]
        t_hbf_free = [t_wo_free[1], t_wo_free[1]]
        t_hfm_free = [t_wo_free[1], t_wo_free[1]]
        t_out_st = []
        t_b2t = None
        t_cb2 = None

        for half in range(2):
            tcs = list(range(8 * half, 8 * half + 8))
            t_cta = []
            for i, tc in enumerate(tcs):
                tmm = O("tensor", lambda e, tc=tc: e.transpose(
                    PD[0:8, 0:128], c_all[:, 8 * tc:8 * (tc + 1)], idf[:, :]),
                    [t_call_tc[tc], pd_free[0], t_rowb_free], inc="T")
                tev = O("vector", lambda e, i=i: e.tensor_copy(
                    cta[0:8, 128 * i:128 * (i + 1)], PD[0:8, 0:128]), [tmm], inc="V")
                pd_free[0] = tev
                t_cta.append(tev)
            t_b2t = DR(lambda e: e.dma_start(out=b2t[:, :], in_=b2_h[:, :]),
                       [t_cb2, t_rowb_free])
            for i, tc in enumerate(tcs):
                tl = DL(lambda e, tc=tc: e.dma_start(
                    out=xst[:, :], in_=x2_h[128 * tc:128 * (tc + 1), :]),
                    [t_xst_free, t_hsb_free[1]])
                tv = O("vector", lambda e, i=i, tc=tc: e.tensor_scalar(
                    yacc[i][:, :], xst[:, :], mu2[:, tc:tc + 1], rs2[:, tc:tc + 1],
                    op0=ALU.subtract, op1=ALU.mult), [tl, t_yacc_free[i]], inc="V")
                t_xst_free = tv
                tmm = None
                for g in range(2):
                    tmm = O("tensor", lambda e, i=i, g=g: e.matmul(
                        PC[:, 512 * g:512 * (g + 1)], cta[0:8, 128 * i:128 * (i + 1)],
                        b2t[:, 512 * g:512 * (g + 1)], start=True, stop=True),
                        [t_cta[i], t_b2t, pc_free[g]], inc="T" if g == 1 else None)
                tv = O("vector", lambda e, i=i: e.tensor_tensor(
                    yacc[i][:, :], yacc[i][:, :], PC[:, :], op=ALU.add), [tmm, tv], inc="V")
                pc_free[0] = pc_free[1] = tv
                t_yacc_free[i] = tv
                t_cb2 = tmm

            for e_ in range(ne_dbg):
                for kc in range(NKC):
                    O("gpsimd", lambda e, e_=e_, kc=kc: e.dma_start(
                        out=w1e[kc][:, :], in_=w1_h[e_, 128 * kc:128 * (kc + 1), :]),
                        [t_w1_free[kc]], inc="dW1", amt=16)
                t_w1_grp = ("dW1", kb.cnt["dW1"])
                for kc in range(NKC):
                    t_w1_dma[kc] = t_w1_grp
                for k in range(NKC):
                    O("gpsimd", lambda e, e_=e_, k=k: e.dma_start(
                        out=arw[k][:, 2048:4096].rearrange("p (two dd) -> p two dd", two=2),
                        in_=w2_h[e_, 256 * k:256 * (k + 1), :].rearrange(
                            "(two p) dd -> p two dd", p=128)),
                        [t_w2_free[2 * k], t_w2_free[2 * k + 1]], inc="dW2", amt=16)
                t_w2_grp = ("dW2", kb.cnt["dW2"])
                for k in range(16):
                    t_w2_dma[k] = t_w2_grp
                t_repl = []
                tprev = [t_cb2, t_rowb_free, t_ofm_free]
                for src_h, dst in ((b1_h, b1r), (l1g_h, gr_), (l1b_h, brr)):
                    tl = DR(lambda e, src_h=src_h, e_=e_: e.dma_start(
                        out=rowb[:, :], in_=src_h[e_:e_ + 1, :]), tprev)
                    tr = repl_rows(dst, rowb, HF, wait=[tl])
                    tprev = [tr]
                    t_repl.append(tr)
                t_rowb_free = t_repl[-1]

                def emit_w1_and_chain(i, tc, t_repl=t_repl):
                    global_state = None
                    pp = i % 2
                    tmm = None
                    for kc in range(NKC):
                        for g in range(4):
                            pq = (PA, PB)[g // 2][:, 512 * (g % 2):512 * (g % 2 + 1)]
                            fr = pa_free[g % 2] if g < 2 else pb_free[g % 2]
                            ws = ([t_w1_dma[kc], t_lnar2_tc[tc], fr] if kc == 0 else [])
                            tmm = O("tensor", lambda e, kc=kc, g=g, tc=tc, pq=pq: e.matmul(
                                pq, lnar[kc][:, 128 * tc:128 * (tc + 1)],
                                w1e[kc][:, 512 * g:512 * (g + 1)],
                                start=(kc == 0), stop=(kc == NKC - 1)),
                                ws,
                                inc="T" if (kc == NKC - 1 and g == 3) else None)
                    hx = hsb[pp]
                    tv = O("vector", lambda e, hx=hx: e.tensor_tensor(
                        hx[:, 0:1024], PA[:, :], b1r[:, 0:1024], op=ALU.add),
                        [tmm, t_repl[0], t_hsb_free[pp]], inc="V")
                    pa_f = tv
                    tv = O("vector", lambda e, hx=hx: e.tensor_tensor(
                        hx[:, 1024:2048], PB[:, :], b1r[:, 1024:2048], op=ALU.add), [tv], inc="V")
                    pb_f = tv
                    return tmm, tv, pa_f, pb_f

                def emit_chain(i, tc, e_, tv, t_repl=t_repl):
                    pp = i % 2
                    hx = hsb[pp]
                    ts_ = ln_stats(hx, HF, sm_c[:, 6:7], sm_c[:, 7:8], wait=[tv])
                    tv = O("vector", lambda e, hx=hx: e.tensor_scalar(
                        hx[:, :], hx[:, :], sm_c[:, 6:7], sm_c[:, 7:8],
                        op0=ALU.subtract, op1=ALU.mult), [ts_], inc="V")
                    tv = O("vector", lambda e, hx=hx: e.tensor_tensor(
                        hx[:, :], hx[:, :], gr_, op=ALU.mult), [tv, t_repl[1]], inc="V")
                    tv = O("vector", lambda e, hx=hx: e.tensor_tensor(
                        hx[:, :], hx[:, :], brr, op=ALU.add), [tv, t_repl[2]], inc="V")
                    tg = O("scalar", lambda e, hx=hx, pp=pp: e.activation(
                        hbf[pp][:, :], hx[:, :], AF.Gelu, bias=0.0, scale=1.0),
                        [tv, t_hbf_free[pp]], inc="A")
                    t_hsb_free[pp] = tg
                    tv = O("vector", lambda e, pp=pp, tc=tc, e_=e_: e.tensor_scalar(
                        hbf[pp][:, :], hbf[pp][:, :], c_all[:, 8 * tc + e_:8 * tc + e_ + 1], None,
                        op0=ALU.mult), [tg, t_call_tc[tc]], inc="V")
                    return tv

                def emit_tail(i, tc, tv_c):
                    nonlocal t_w2_last
                    pp = i % 2
                    tev = None
                    for g in range(2):
                        tmm = None
                        for j in range(8):
                            tmm = O("tensor", lambda e, g=g, j=j, pp=pp: e.transpose(
                                PT[:, 128 * j:128 * (j + 1)],
                                hbf[pp][:, 1024 * g + 128 * j:1024 * g + 128 * (j + 1)], idb[:, :]),
                                [tv_c, pt_free[0]] if j == 0 else [],
                                inc="T" if j == 7 else None)
                        tev = O("vector", lambda e, g=g, pp=pp: e.tensor_copy(
                            hfm[pp][:, 1024 * g:1024 * (g + 1)], PT[:, :]),
                            [tmm, t_hfm_free[pp]], inc="V")
                        pt_free[0] = tev
                    t_hbf_free[pp] = tmm
                    tmm = None
                    for k in range(16):
                        for g in range(2):
                            ws = ([tev, t_w2_dma[k], pc_free[g]] if k == 0 else [])
                            tmm = O("tensor", lambda e, k=k, g=g, pp=pp: e.matmul(
                                PC[:, 512 * g:512 * (g + 1)],
                                hfm[pp][:, 128 * k:128 * (k + 1)],
                                w2e[k][:, 512 * g:512 * (g + 1)],
                                start=(k == 0), stop=(k == 15)),
                                ws,
                                inc="T" if (k == 15 and g == 1) else None)
                    t_hfm_free[pp] = tmm
                    t_w2_last = tmm
                    return tmm, pp

                def emit_yevac(i, tmm):
                    tv = O("vector", lambda e, i=i: e.tensor_tensor(
                        yacc[i][:, :], yacc[i][:, :], PC[:, :], op=ALU.add),
                        [tmm, t_yacc_free[i]], inc="V")
                    pc_free[0] = pc_free[1] = tv
                    t_yacc_free[i] = tv

                t_w2_last = None
                pend = None
                for i, tc in enumerate(tcs):
                    tmm1, tvh, pa_f, pb_f = emit_w1_and_chain(i, tc)
                    t_w1_last = tmm1
                    if pend is not None:
                        j, tvc_j = pend
                        tmm2, _ = emit_tail(j, tcs[j], tvc_j)
                    tv_c = emit_chain(i, tc, e_, tvh)
                    pa_free[0] = pa_free[1] = pa_f
                    pb_free[0] = pb_free[1] = pb_f
                    if pend is not None:
                        j, _ = pend
                        emit_yevac(j, tmm2)
                    pend = (i, tv_c)
                j, tvc_j = pend
                tmm2, _ = emit_tail(j, tcs[j], tvc_j)
                emit_yevac(j, tmm2)
                t_w1_free = [t_w1_last] * NKC
                t_w2_free = [t_w2_last] * 16

            if dbg == 2:
                continue
            tl = DR(lambda e: e.dma_start(out=rowb[:, 0:1024], in_=lng_h[:, :]),
                    [t_rowb_free])
            tg1 = repl_rows(lngr, rowb[:, 0:1024], 1024,
                            wait=[tl, t_hfm_free[0], t_hfm_free[1]])
            tl = DR(lambda e: e.dma_start(out=rowb[:, 1024:2048], in_=lnb_h[:, :]))
            tg2 = repl_rows(lnbr, rowb[:, 1024:2048], 1024, wait=[tl])
            t_rowb_free = tg2
            for i, tc in enumerate(tcs):
                pp = i % 2
                tl = DL(lambda e, tc=tc: e.dma_start(
                    out=xst[:, :], in_=x2_h[128 * tc:128 * (tc + 1), :]),
                    [t_xst_free, t_hsb_free[0], t_hsb_free[1]])
                ts_ = ln_stats(yacc[i], D, sm_c[:, 6:7], sm_c[:, 7:8], wait=[t_yacc_free[i]])
                tv = O("vector", lambda e, i=i, pp=pp: e.tensor_scalar(
                    outsc[pp][:, :], yacc[i][:, :], sm_c[:, 6:7], sm_c[:, 7:8],
                    op0=ALU.subtract, op1=ALU.mult),
                    [ts_, t_hbf_free[pp]], inc="V")
                tv = O("vector", lambda e, pp=pp: e.tensor_tensor(
                    outsc[pp][:, :], outsc[pp][:, :], lngr, op=ALU.mult), [tv, tg1], inc="V")
                tv = O("vector", lambda e, pp=pp: e.tensor_tensor(
                    outsc[pp][:, :], outsc[pp][:, :], lnbr, op=ALU.add), [tv, tg2], inc="V")
                tv = O("vector", lambda e, pp=pp: e.tensor_tensor(
                    outsc[pp][:, :], outsc[pp][:, :], xst[:, :], op=ALU.add), [tv, tl], inc="V")
                t_xst_free = tv
                tst = DX(lambda e, tc=tc, pp=pp: e.dma_start(
                    out=out_h[128 * tc:128 * (tc + 1), :], in_=outsc[pp][:, :]), [tv])
                t_out_st.append(tst)
                t_hbf_free[pp] = tst

        DX(lambda e: e.dma_start(out=x2o_h[:, :], in_=x2_h[:, :]), [t_x2_done])
        if dbg == 2:
            t_out_st = [t_yacc_free[7]]
            for j in range(8):
                DX(lambda e, j=j: e.dma_start(out=dbg_h["d_yacc"][:, 1024 * j:1024 * (j + 1)],
                                              in_=yacc[j][:, :]), [t_out_st[-1]])
            DX(lambda e: e.dma_start(out=dbg_h["d_lngr"][:, :], in_=lngr[:, :]), [t_out_st[-1]])
            DX(lambda e: e.dma_start(out=dbg_h["d_lnbr"][:, :], in_=lnbr[:, :]), [t_out_st[-1]])
            DX(lambda e: e.dma_start(out=dbg_h["d_b1r"][:, :], in_=b1r[:, :]), [t_out_st[-1]])
            DX(lambda e: e.dma_start(out=dbg_h["d_hsb0"][:, :], in_=hsb[0][:, :]), [t_out_st[-1]])
            DX(lambda e: e.dma_start(out=dbg_h["d_hbf0"][:, :], in_=hbf[0][:, :]), [t_out_st[-1]])
            DX(lambda e: e.dma_start(out=dbg_h["d_hfm0"][:, :], in_=hfm[0][:, :]), [t_out_st[-1]])
            for k in range(16):
                DX(lambda e, k=k: e.dma_start(out=dbg_h["d_w2e"][:, 1024 * k:1024 * (k + 1)],
                                              in_=w2e[k][:, :]), [t_out_st[-1]])
            for k in range(8):
                DX(lambda e, k=k: e.dma_start(out=dbg_h["d_w1e"][:, 2048 * k:2048 * (k + 1)],
                                              in_=w1e[k][:, :]), [t_out_st[-1]])
            DX(lambda e: e.dma_start(out=dbg_h["d_gr"][:, :], in_=gr_[:, :]), [t_out_st[-1]])
            DX(lambda e: e.dma_start(out=dbg_h["d_brr"][:, :], in_=brr[:, :]), [t_out_st[-1]])
            DX(lambda e: e.dma_start(out=dbg_h["d_call"][:, :], in_=c_all[:, :]), [t_out_st[-1]])
            DX(lambda e: e.dma_start(out=dbg_h["d_mu2"][:, :], in_=mu2[:, :]), [t_out_st[-1]])
            DX(lambda e: e.dma_start(out=dbg_h["d_rs2"][:, :], in_=rs2[:, :]), [t_out_st[-1]])
            for k in range(NKC):
                DX(lambda e, k=k: e.dma_start(out=dbg_h["d_ln2"][128 * k:128 * (k + 1), :],
                                              in_=lnar[k][:, :]), [t_out_st[-1]])
        kb.wait("sync", [("dX", kb.cnt["dX"]), ("dL", kb.cnt["dL"]),
                         ("dR", kb.cnt["dR"]), ("dT", kb.cnt["dT"])])
        kb.wait("gpsimd", [("dW", kb.cnt["dW"]), ("dW1", kb.cnt["dW1"]),
                           ("dW2", kb.cnt["dW2"])])
        kb.emit_all(block)
    return nc


_CACHE = {}


def kernel(**inputs):
    if "nc" not in _CACHE:
        _CACHE["nc"] = build_nc()
    nc = _CACHE["nc"]
    f32 = np.float32
    x = np.ascontiguousarray(inputs["L0_attn_x"], dtype=f32)
    base = dict(
        wqkv=np.ascontiguousarray(inputs["L0_attn_Wqkv"][0:2], dtype=f32),
        bqkv=np.ascontiguousarray(inputs["L0_attn_bqkv"][0:2], dtype=f32),
        wout=np.ascontiguousarray(inputs["L0_attn_Wout"][0:2], dtype=f32),
        bout=np.ascontiguousarray(inputs["L0_attn_bout"][0:2], dtype=f32),
        gate_w=np.ascontiguousarray(inputs["L0_ffn_gate_w"], dtype=f32),
        w1=np.ascontiguousarray(inputs["L0_ffn_w1"], dtype=f32),
        b1=np.ascontiguousarray(inputs["L0_ffn_b1"], dtype=f32),
        ln1g=np.ascontiguousarray(inputs["L0_ffn_ln1_g"], dtype=f32),
        ln1b=np.ascontiguousarray(inputs["L0_ffn_ln1_b"], dtype=f32),
        w2=np.ascontiguousarray(inputs["L0_ffn_w2"], dtype=f32),
        b2=np.ascontiguousarray(inputs["L0_ffn_b2"], dtype=f32),
        lng=np.ascontiguousarray(inputs["L0_ffn_ln_g"], dtype=f32).reshape(1, D),
        lnb=np.ascontiguousarray(inputs["L0_ffn_ln_b"], dtype=f32).reshape(1, D),
        ident=np.eye(P, dtype=f32),
    )
    in_maps = [dict(base, x=np.ascontiguousarray(x[b])) for b in range(8)]
    res = run_bass_kernel_spmd(nc, in_maps, core_ids=list(range(8)))
    out = np.stack([np.asarray(r["out"]) for r in res.results], axis=0)
    return out.astype(np.float32)
